# version 01
# baseline (speedup 1.0000x reference)
import numpy as np
import ml_dtypes
from contextlib import ExitStack

import concourse.bass as bass
import concourse.mybir as mybir
from concourse.bass_utils import run_bass_kernel_spmd

F32 = mybir.dt.float32
BF16 = mybir.dt.bfloat16
AF = mybir.ActivationFunctionType
ALU = mybir.AluOpType
AX = mybir.AxisListType
BFNP = ml_dtypes.bfloat16

NCORE = 8
D = 4096
NC = 32
T = 512
NT = 4
INV_D = 1.0 / 4096.0
EPS = 1e-5
ISQRT_DK = float(1.0 / np.sqrt(32.0))


class Tok:
    __slots__ = ("sem", "val")

    def __init__(self, sem, val):
        self.sem = sem
        self.val = val


class Res:
    __slots__ = ("w", "rs")

    def __init__(self):
        self.w = None
        self.rs = []

    def wdeps(self):
        d = list(self.rs)
        if self.w is not None:
            d.append(self.w)
        return d

    def rdeps(self):
        return [self.w] if self.w is not None else []

    def set_w(self, t):
        self.w = t
        self.rs = []

    def add_r(self, t):
        if t is not None:
            self.rs.append(t)


def _consts():
    c = {}
    c["id8"] = np.eye(8, dtype=np.float32)
    c["id128b"] = np.eye(128, dtype=BFNP)
    c["lncol"] = np.full((128, 1), INV_D, dtype=BFNP)
    c["onescol"] = np.full((128, 1), 1.0, dtype=BFNP)
    c["onesrow128"] = np.full((1, 128), 1.0, dtype=np.float32)
    c["ones8row"] = np.full((1, 8), 1.0, dtype=np.float32)
    psel = np.zeros((8, 128), dtype=BFNP)
    for e in range(8):
        psel[e, e * 16:(e + 1) * 16] = 1.0
    c["psel"] = psel
    sel = np.zeros((2, 128, 8), dtype=BFNP)
    for ch in range(2):
        for e in range(4):
            sel[ch, e * 32:(e + 1) * 32, ch * 4 + e] = 1.0
    c["sel"] = sel
    return c


def build(cfg=None):
    cfg = dict(cfg or {})
    subs_en = {"ffn1": cfg.get("ffn1", True), "attn": cfg.get("attn", True),
               "ffn2": cfg.get("ffn2", True)}
    ntiles = cfg.get("ntiles", NT)

    nc = bass.Bass("TRN2", target_bir_lowering=False, debug=False)
    es = ExitStack()

    xin = nc.dram_tensor("xin", [NT, NC, 128, T], F32, kind="ExternalInput").ap()
    wbD = nc.dram_tensor("wb", [32, 128, 32, 128], F32, kind="ExternalInput").ap()
    w1bD = nc.dram_tensor("w1b", [32, 128, 32, 128], F32, kind="ExternalInput").ap()
    w2tD = nc.dram_tensor("w2tb", [32, 128, T], F32, kind="ExternalInput").ap()
    maskD = nc.dram_tensor("maskb", [32, 128, T], F32, kind="ExternalInput").ap()
    wqD = nc.dram_tensor("wqb", [2, 128, 32, 128], F32, kind="ExternalInput").ap()
    wkD = nc.dram_tensor("wkb", [2, 128, 32, 128], F32, kind="ExternalInput").ap()
    abD = nc.dram_tensor("ab", [128, 32, 128], F32, kind="ExternalInput").ap()
    bmD = nc.dram_tensor("bmb", [128, 32, 128], F32, kind="ExternalInput").ap()
    browD = nc.dram_tensor("brow", [1, D], F32, kind="ExternalInput").ap()
    b2rowD = nc.dram_tensor("b2row", [1, D], F32, kind="ExternalInput").ap()
    yout = nc.dram_tensor("yout", [NT, NC, 128, T], F32, kind="ExternalOutput").ap()

    wTbf = nc.dram_tensor("wTbf", [32, 128, 32, 128], BF16, kind="Internal").ap()
    bar_in = nc.dram_tensor("bar_in", [1, 256], F32, kind="Internal")
    bar_out = nc.dram_tensor("bar_out", [1, 256], F32, kind="Internal")
    shard = nc.dram_tensor("attn_shard", [T, D], BF16, kind="Internal")
    attnM = nc.dram_tensor("attnM", [D, D], BF16, kind="Internal")
    wqbf = nc.dram_tensor("wqbf", [2, 128, 32, 128], BF16, kind="Internal").ap()
    wkbf = nc.dram_tensor("wkbf", [2, 128, 32, 128], BF16, kind="Internal").ap()

    CN = _consts()
    cD = {k: nc.inline_tensor(v, name="c_" + k).ap() for k, v in CN.items()}

    S = lambda *a: es.enter_context(nc.sbuf_tensor(*a))
    x_sb = S("x", [128, NC, T], F32)
    xb_sb = S("xb", [128, NC, T], BF16)
    ws_sb = S("wstream", [128, 3, 32, 128], BF16)
    a_sb = S("a_res", [128, 32, 128], BF16)
    bm_sb = S("bm_res", [128, 32, 128], BF16)
    u_sb = S("u_t", [128, 4, T], F32)
    rb_sb = S("rstdb", [128, 2, T], F32)
    q_sb = S("q_t", [128, 2, T], F32)
    xsq_sb = S("xsq", [128, 2, T], BF16)
    prod_sb = S("prod", [128, 2, T], BF16)
    h_sb = S("h_t", [128, T], BF16)
    g_sb = S("g_t", [128, T], BF16)
    brow_sb = S("brow_sb", [128, D], BF16)
    b2row_sb = S("b2row_sb", [128, D], BF16)
    wrow_sb = S("wrow_sb", [128, D], BF16)
    acol_sb = S("acol_sb", [128, D], BF16)
    qrow_sb = S("qrow_sb", [128, 640], BF16)
    stat = S("stat", [128, 3 * T], F32)
    statb = S("statb", [128, 2 * T], BF16)
    r2s = S("r2s", [128, T], F32)
    wrk = stat
    sc_sb = S("sc", [8, T], F32)
    wfm_sb = S("wfm", [8, T], BF16)
    mx_sb = S("mx", [128, 4], F32)
    sm_sb = S("sm", [128, 8], F32)
    esc_sb = S("esc", [128, 32], F32)
    wtok_sb = S("wtok", [128, 32], BF16)
    eps_sb = S("epsc", [128, 1], F32)
    id8_sb = S("id8_sb", [8, 8], F32)
    id128b_sb = S("id128b_sb", [128, 128], BF16)
    lncol_sb = S("lncol_sb", [128, 1], BF16)
    onescol_sb = S("onescol_sb", [128, 1], BF16)
    onesrow_sb = S("onesrow_sb", [1, 128], F32)
    ones8row_sb = S("ones8row_sb", [1, 8], F32)
    psel_sb = S("psel_sb", [8, 128], BF16)
    sel_sb = S("sel_sb", [128, 2, 8], BF16)
    w2t_sb = xb_sb
    mask_sb = q_sb
    stage_sb = prod_sb

    P = [es.enter_context(nc.psum_tensor("pb%d" % i, [128, T], F32)) for i in range(8)]

    pe, act, dve, pool, sp = nc.tensor, nc.scalar, nc.vector, nc.gpsimd, nc.sync
    gp = pool

    sems, cnt, own, seen = {}, {}, {}, {}

    def mksem(name, owner=None):
        sems[name] = es.enter_context(nc.semaphore(name))
        cnt[name] = 0
        own[name] = owner

    for nm, ow in (("sQ", None), ("sS", None), ("sG", None), ("sCC", None),
                   ("sPE", pe), ("sA", act), ("sV", dve), ("sP", pool)):
        mksem(nm, ow)

    def inc(ins, name, n=1):
        ins.then_inc(sems[name], n)
        cnt[name] += n
        return Tok(name, cnt[name])

    def wait(eng, deps):
        best = {}
        for t in deps:
            if t is not None and t.val > best.get(t.sem, -1):
                best[t.sem] = t.val
        sn = seen.setdefault(id(eng), {})
        for s, v in best.items():
            if sn.get(s, -1) >= v:
                continue
            eng.wait_ge(sems[s], v)
            sn[s] = v

    def dma(q, qsem, out, in_, deps=()):
        wait(q, list(deps))
        return inc(q.dma_start(out=out, in_=in_), qsem, 16)

    PSUM_DELAY = int(cfg.get("delay", 0))

    def pdelay(eng):
        if PSUM_DELAY:
            eng.nop(cycle_cnt=PSUM_DELAY)

    bank = [Res() for _ in range(8)]
    ws_slot = [Res() for _ in range(3)]
    u_slot = [Res() for _ in range(4)]
    q_slot = [Res() for _ in range(2)]
    xsq_slot = [Res() for _ in range(2)]
    prod_slot = [Res() for _ in range(2)]
    mask_slot = [Res() for _ in range(2)]
    stage_slot = [Res() for _ in range(2)]
    wtb_blk = [Res() for _ in range(32)]
    xchunk = [Res() for _ in range(NC)]
    xbchunk = [Res() for _ in range(NC)]
    _misc = {}

    def R(name):
        return _misc.setdefault(name, Res())

    def xb_read_deps():
        return [xbchunk[k].w for k in range(NC)]

    def xb_mark_read(t):
        for k in range(NC):
            xbchunk[k].add_r(t)

    def mm_group(bki, out_ap, items):
        bk = bank[bki]
        ins = None
        n = len(items)
        for i, (l, r, dd) in enumerate(items):
            d = list(dd)
            if i == 0:
                d += bk.wdeps()
            wait(pe, d)
            ins = pe.matmul(out=out_ap, lhsT=l, rhs=r, start=(i == 0), stop=(i == n - 1))
        t = inc(ins, "sPE")
        bk.set_w(t)
        return t

    for dst, src in ((id8_sb[:], cD["id8"]), (id128b_sb[:], cD["id128b"]),
                     (lncol_sb[:], cD["lncol"]), (onescol_sb[:], cD["onescol"]),
                     (onesrow_sb[:], cD["onesrow128"]), (ones8row_sb[:], cD["ones8row"]),
                     (psel_sb[:], cD["psel"]), (sel_sb[:, 0, :], cD["sel"][0]),
                     (sel_sb[:, 1, :], cD["sel"][1])):
        t_consts = dma(sp, "sQ", dst, src)
    t_brow = dma(gp, "sG", brow_sb[0:1, :], browD)
    R("brow").set_w(t_brow)
    t_b2row = dma(gp, "sG", b2row_sb[0:1, :], b2rowD)
    R("b2row").set_w(t_b2row)
    t_a = dma(gp, "sG", a_sb[:], abD)
    t_bm = dma(gp, "sG", bm_sb[:], bmD)
    t_wqc = dma(gp, "sG", wqbf, wqD)
    t_wkc = dma(gp, "sG", wkbf, wkD)
    t_w2t = None
    for i in range(4):
        t_w2t = dma(gp, "sG", w2t_sb[:, i * 8:(i + 1) * 8, :],
                    w2tD[i * 8:(i + 1) * 8].rearrange("k p t -> p k t"))
    for k in range(NC):
        xbchunk[k].set_w(t_w2t)
    for i in range(8):
        tk = dma(sp, "sQ", x_sb[:, i * 4:(i + 1) * 4, :],
                 xin[0, i * 4:(i + 1) * 4].rearrange("c p t -> p c t"))
        for c in range(i * 4, (i + 1) * 4):
            xchunk[c].set_w(tk)
    ins = dve.memset(eps_sb[:], EPS)
    t_eps = inc(ins, "sV")
    R("eps").set_w(t_eps)

    t_cc = None
    if subs_en["attn"]:
        for m in range(32):
            sl = m % 2
            t_w1 = dma(gp, "sG", ws_sb[:, sl, :, :], w1bD[m], deps=ws_slot[sl].wdeps())
            ws_slot[sl].set_w(t_w1)
            t_m = dma(sp, "sQ", mask_sb[:, sl, :], maskD[m], deps=q_slot[sl].wdeps())
            q_slot[sl].set_w(t_m)
            items = [(ws_sb[:, sl, k, :], w2t_sb[:, k, :],
                      ([t_w1, t_w2t] if k == 0 else [])) for k in range(32)]
            t_pe = mm_group(m % 2, P[m % 2][:], items)
            ws_slot[sl].add_r(t_pe)
            ssl = prod_slot[m % 2]
            wait(dve, [t_pe, t_m] + ssl.wdeps())
            pdelay(dve)
            ins = dve.tensor_tensor(out=stage_sb[:, m % 2, :], in0=P[m % 2][:],
                                    in1=mask_sb[:, m % 2, :], op=ALU.mult)
            t_st = inc(ins, "sV")
            ssl.set_w(t_st)
            bank[m % 2].add_r(t_st)
            q_slot[sl].add_r(t_st)
            for ml in range(4):
                t_sd = dma(sp, "sS" if False else "sQ",
                           shard.ap()[ml * 128:(ml + 1) * 128, m * 128:(m + 1) * 128],
                           stage_sb[:, m % 2, ml * 128:(ml + 1) * 128], deps=[t_st])
                ssl.add_r(t_sd)
        def emit_collective():
            wait(gp, [Tok("sQ", cnt["sQ"])])
            ins = gp.collective_compute("AllGather", ALU.bypass,
                                        replica_groups=[list(range(NCORE))],
                                        ins=[shard.ap().opt()], outs=[attnM.ap().opt()])
            inc(ins, "sCC")
            ins = gp.collective_compute("AllReduce", ALU.add,
                                        replica_groups=[list(range(NCORE))],
                                        ins=[bar_in.ap().opt()], outs=[bar_out.ap().opt()])
            return inc(ins, "sCC")
        if not cfg.get("late_cc", True):
            t_cc = emit_collective()
    if cnt["sPE"]:
        for k in range(NC):
            xbchunk[k].add_r(Tok("sPE", cnt["sPE"]))

    def xb_cast_chunk(m, deps=()):
        eng, sname = (act, "sA") if m % 2 == 0 else (pool, "sP")
        wait(eng, xchunk[m].rdeps() + xbchunk[m].wdeps() + list(deps))
        ins = eng.tensor_copy(out=xb_sb[:, m, :], in_=x_sb[:, m, :]) \
            if eng is pool else eng.activation(out=xb_sb[:, m, :], in_=x_sb[:, m, :],
                                               func=AF.Copy)
        t = inc(ins, sname)
        xbchunk[m].set_w(t)
        xchunk[m].add_r(t)
        xs = xsq_slot[m % 2]
        wait(dve, [t] + xs.wdeps())
        ins = dve.tensor_tensor(out=xsq_sb[:, m % 2, :], in0=xb_sb[:, m, :],
                                in1=xb_sb[:, m, :], op=ALU.mult)
        t2 = inc(ins, "sV")
        xs.set_w(t2)
        xbchunk[m].add_r(t2)

    def post_ln():
        for m in range(NC):
            xb_cast_chunk(m)
            ln_sums_chunk(m, first=(m == 0), last=(m == NC - 1))

    def ln_sums_chunk(m, first, last):
        d = [xbchunk[m].w, xsq_slot[m % 2].w]
        if first:
            d += bank[4].wdeps() + bank[5].wdeps()
        wait(pe, d)
        pe.matmul(out=P[4][0:1, :], lhsT=lncol_sb[:], rhs=xb_sb[:, m, :],
                  start=first, stop=last)
        i2 = pe.matmul(out=P[5][0:1, :], lhsT=lncol_sb[:], rhs=xsq_sb[:, m % 2, :],
                       start=first, stop=last)
        t = inc(i2, "sPE")
        xbchunk[m].add_r(t)
        xsq_slot[m % 2].add_r(t)
        if first or last:
            bank[4].set_w(t)
            bank[5].set_w(t)

    def act_row(dst_res, out_ap, in_ap, src_res, func=AF.Copy, bias=0.0, scale=1.0,
                extra=()):
        wait(act, dst_res.wdeps() + src_res.rdeps() + list(extra))
        ins = act.activation(out=out_ap, in_=in_ap, func=func, bias=bias, scale=scale)
        t = inc(ins, "sA")
        dst_res.set_w(t)
        src_res.add_r(t)
        return t

    def ln_finalize(nxt, t0_meanb=False):
        MEAN = slice(0, T)
        WORK = slice(T, 2 * T)
        RSTD = slice(2 * T, 3 * T)
        pdelay(act)
        t_mean = act_row(R("st_mean"), stat[0:1, MEAN], P[4][0:1, :], bank[4])
        t_ex2 = act_row(R("st_work"), stat[0:1, WORK], P[5][0:1, :], bank[5])
        t_mneg = act_row(R("statb_m"), statb[0:1, 0:T], stat[0:1, MEAN], R("st_mean"),
                         scale=-1.0)
        t_m2 = act_row(R("st_rstd"), stat[0:1, RSTD], stat[0:1, MEAN], R("st_mean"),
                       func=AF.Square)
        wait(dve, [t_m2, t_ex2] + R("st_work").rs + R("st_rstd").rs)
        ins = dve.tensor_tensor(out=stat[0:1, WORK], in0=stat[0:1, WORK],
                                in1=stat[0:1, RSTD], op=ALU.subtract)
        t_var = inc(ins, "sV")
        R("st_work").set_w(t_var)
        R("st_rstd").add_r(t_var)
        t_sd = act_row(R("st_work"), stat[0:1, WORK], stat[0:1, WORK], R("st_work"),
                       func=AF.Sqrt, bias=eps_sb[0:1, 0:1], extra=R("eps").rdeps())
        t_sdbf = act_row(R("statb_s"), statb[0:1, T:2 * T], stat[0:1, WORK],
                         R("st_work"))
        wait(dve, [t_sd] + R("st_rstd").wdeps())
        ins = dve.reciprocal(out=stat[0:1, RSTD], in_=stat[0:1, WORK])
        t_rstd = inc(ins, "sV")
        R("st_rstd").set_w(t_rstd)
        R("st_work").add_r(t_rstd)
        toks = {"mneg": t_mneg, "sdbf": t_sdbf}
        if nxt == "ffn":
            wait(dve, [t_rstd] + R("r2s").wdeps())
            ins = dve.tensor_tensor(out=r2s[0:1, :], in0=stat[0:1, RSTD],
                                    in1=stat[0:1, RSTD], op=ALU.mult)
            t_r2a = inc(ins, "sV")
            R("r2s").set_w(t_r2a)
            R("st_rstd").add_r(t_r2a)
            t_r2 = act_row(R("r2s"), r2s[0:1, :], r2s[0:1, :], R("r2s"),
                           scale=ISQRT_DK)
            toks["r2s"] = t_r2
            t_rh = act_row(R("st_work"), stat[0:1, WORK], stat[0:1, RSTD],
                           R("st_rstd"), scale=0.5)
            src, src_res = stat[0:1, WORK], R("st_work")
        else:
            src, src_res = stat[0:1, RSTD], R("st_rstd")
        slot = 0 if nxt == "ffn" else 1
        wait(pe, bank[7].wdeps() + src_res.rdeps())
        ins = pe.matmul(out=P[7][:], lhsT=onesrow_sb[0:1, :], rhs=src,
                        start=True, stop=True)
        t_bc = inc(ins, "sPE")
        bank[7].set_w(t_bc)
        src_res.add_r(t_bc)
        rbs = R("rb%d" % slot)
        wait(act, [t_bc] + rbs.wdeps())
        pdelay(act)
        ins = act.activation(out=rb_sb[:, slot, :], in_=P[7][:], func=AF.Copy)
        t_rb = inc(ins, "sA")
        rbs.set_w(t_rb)
        bank[7].add_r(t_rb)
        toks["rb_slot"] = slot
        if t0_meanb:
            wait(pe, bank[7].wdeps() + R("st_mean").rdeps())
            ins = pe.matmul(out=P[7][:], lhsT=onesrow_sb[0:1, :], rhs=stat[0:1, 0:T],
                            start=True, stop=True)
            t_bc2 = inc(ins, "sPE")
            bank[7].set_w(t_bc2)
            R("st_mean").add_r(t_bc2)
            mbslot = 1 - slot
            mb = R("rb%d" % mbslot)
            wait(act, [t_bc2] + mb.wdeps())
            pdelay(act)
            ins = act.activation(out=rb_sb[:, mbslot, :], in_=P[7][:], func=AF.Copy)
            t_mb = inc(ins, "sA")
            mb.set_w(t_mb)
            bank[7].add_r(t_mb)
            toks["mean_b"] = t_mb
            toks["mb_slot"] = mbslot
        return toks

    def xz_pass(lt):
        mbslot = lt["mb_slot"]
        for m in range(NC):
            wait(dve, xbchunk[m].wdeps() + [lt["mean_b"]])
            ins = dve.tensor_tensor(out=xb_sb[:, m, :], in0=xb_sb[:, m, :],
                                    in1=rb_sb[:, mbslot, :], op=ALU.subtract)
            t = inc(ins, "sV")
            xbchunk[m].set_w(t)
            R("rb%d" % mbslot).add_r(t)

    def small_phase(lt, skip_mean, build_qrows):
        mneg_dep = [] if skip_mean else [lt["mneg"], R("qrow").w]
        slmap = [0, 1, 2, 0]
        srcs = [(wqbf, 0, t_wqc), (wqbf, 1, t_wqc), (wkbf, 0, t_wkc), (wkbf, 1, t_wkc)]
        tw = [None] * 4
        for j in (0, 1, 2):
            srcD, mm, tc = srcs[j]
            sl = slmap[j]
            tw[j] = dma(sp, "sQ", ws_sb[:, sl, :, :], srcD[mm],
                        deps=ws_slot[sl].wdeps() + [tc])
            ws_slot[sl].set_w(tw[j])
        for j in range(4):
            if j == 3:
                srcD, mm, tc = srcs[3]
                sl = slmap[3]
                tw[3] = dma(sp, "sQ", ws_sb[:, sl, :, :], srcD[mm],
                            deps=ws_slot[sl].wdeps() + [tc])
                ws_slot[sl].set_w(tw[3])
            sl = slmap[j]
            bki = 2 + j
            items = [(ws_sb[:, sl, k, :], xb_sb[:, k, :],
                      ([tw[j]] + xb_read_deps() if k == 0 else [])) for k in range(32)]
            if not skip_mean:
                off = (0 if j < 2 else 256) + (j % 2) * 128
                items.append((qrow_sb[0:1, off:off + 128], statb[0:1, 0:T], mneg_dep))
            t_g = mm_group(bki, P[bki][:], items)
            ws_slot[sl].add_r(t_g)
            xb_mark_read(t_g)
            if build_qrows:
                items2 = [(onescol_sb[:], ws_sb[:, sl, 4 * gg:4 * gg + 4, :], [])
                          for gg in range(8)]
                t_r = mm_group(7, P[7][0:1, 0:T], items2)
                ws_slot[sl].add_r(t_r)
                off = (0 if j < 2 else 256) + (j % 2) * 128
                fold_row(t_r, qrow_sb, "qrow", off)
        items = [(a_sb[:, k, :], xb_sb[:, k, :], ([t_a] if k == 0 else []))
                 for k in range(32)]
        if not skip_mean:
            items.append((qrow_sb[0:1, 512:640], statb[0:1, 0:T], mneg_dep))
        t_h = mm_group(6, P[6][:], items)
        xb_mark_read(t_h)
        if build_qrows:
            items2 = [(onescol_sb[:], a_sb[:, 4 * gg:4 * gg + 4, :], []) for gg in range(8)]
            t_r = mm_group(7, P[7][0:1, 0:T], items2)
            fold_row(t_r, qrow_sb, "qrow", 512)
        wait(act, [t_h] + R("h").wdeps())
        pdelay(act)
        ins = act.activation(out=h_sb[:], in_=P[6][:], func=AF.Copy)
        t_hs = inc(ins, "sA")
        R("h").set_w(t_hs)
        bank[6].add_r(t_hs)
        wait(pe, bank[6].wdeps() + R("r2s").rdeps())
        ins = pe.matmul(out=P[6][0:8, :], lhsT=ones8row_sb[0:1, :], rhs=r2s[0:1, :],
                        start=True, stop=True)
        t_r2b = inc(ins, "sPE")
        bank[6].set_w(t_r2b)
        R("r2s").add_r(t_r2b)
        for mm in range(2):
            qs = q_slot[mm]
            wait(act, bank[2 + mm].rdeps() + qs.wdeps())
            pdelay(act)
            ins = act.activation(out=q_sb[:, mm, :], in_=P[2 + mm][:], func=AF.Copy)
            t_q = inc(ins, "sA")
            qs.set_w(t_q)
            bank[2 + mm].add_r(t_q)
        for mm in range(2):
            ps = prod_slot[mm]
            wait(dve, q_slot[mm].rdeps() + bank[4 + mm].rdeps() + ps.wdeps())
            pdelay(dve)
            ins = dve.tensor_tensor(out=prod_sb[:, mm, :], in0=q_sb[:, mm, :],
                                    in1=P[4 + mm][:], op=ALU.mult)
            t_p = inc(ins, "sV")
            ps.set_w(t_p)
            q_slot[mm].add_r(t_p)
            bank[4 + mm].add_r(t_p)
        items = [(sel_sb[:, mm, :], prod_sb[:, mm, :], prod_slot[mm].rdeps())
                 for mm in range(2)]
        t_sc = mm_group(7, P[7][0:8, :], items)
        for mm in range(2):
            prod_slot[mm].add_r(t_sc)
        wait(act, [t_sc] + R("scs").wdeps())
        pdelay(act)
        ins = act.activation(out=sc_sb[:], in_=P[7][0:8, :], func=AF.Copy)
        t_s1 = inc(ins, "sA")
        R("scs").set_w(t_s1)
        bank[7].add_r(t_s1)
        wait(dve, [t_s1, t_r2b])
        pdelay(dve)
        ins = dve.tensor_tensor(out=sc_sb[:], in0=sc_sb[:], in1=P[6][0:8, :],
                                op=ALU.mult)
        t_s2 = inc(ins, "sV")
        R("scs").set_w(t_s2)
        bank[6].add_r(t_s2)
        wait(pe, bank[2].wdeps() + [t_s2, t_consts])
        for c in range(4):
            ins = pe.transpose(out=P[2][:, c * 8:(c + 1) * 8],
                               in_=sc_sb[0:8, c * 128:(c + 1) * 128], identity=id8_sb[:])
        t_tr = inc(ins, "sPE")
        bank[2].set_w(t_tr)
        R("scs").add_r(t_tr)
        wait(dve, [t_tr] + R("mx").wdeps())
        pdelay(dve)
        for c in range(4):
            ins = dve.tensor_reduce(out=mx_sb[:, c:c + 1], in_=P[2][:, c * 8:(c + 1) * 8],
                                    axis=AX.X, op=ALU.max, negate=True)
        t_mx = inc(ins, "sV")
        R("mx").set_w(t_mx)
        bank[2].add_r(t_mx)
        wait(act, [t_mx] + R("esc").wdeps())
        pdelay(act)
        for c in range(4):
            ins = act.activation(out=esc_sb[:, c * 8:(c + 1) * 8],
                                 in_=P[2][:, c * 8:(c + 1) * 8], func=AF.Exp,
                                 bias=mx_sb[:, c:c + 1], scale=1.0)
        t_e = inc(ins, "sA")
        R("esc").set_w(t_e)
        R("mx").add_r(t_e)
        bank[2].add_r(t_e)
        wait(dve, [t_e] + R("sm").wdeps())
        for c in range(4):
            ins = dve.tensor_reduce(out=sm_sb[:, c:c + 1], in_=esc_sb[:, c * 8:(c + 1) * 8],
                                    axis=AX.X, op=ALU.add)
        t_sums = inc(ins, "sV")
        wait(dve, [t_sums])
        ins = dve.reciprocal(out=sm_sb[:, 4:8], in_=sm_sb[:, 0:4])
        t_rec = inc(ins, "sV")
        wait(dve, [t_rec])
        for c in range(4):
            ins = dve.tensor_scalar(out=wtok_sb[:, c * 8:(c + 1) * 8],
                                    in0=esc_sb[:, c * 8:(c + 1) * 8],
                                    scalar1=sm_sb[:, 4 + c:5 + c], scalar2=None,
                                    op0=ALU.mult)
        t_w = inc(ins, "sV")
        R("sm").set_w(t_w)
        R("esc").add_r(t_w)
        R("wtok").set_w(t_w)
        p3b = P[3][:].bitcast(BF16)
        wait(pe, bank[3].wdeps() + [t_w])
        for c in range(4):
            ins = pe.transpose(out=p3b[0:8, c * 128:(c + 1) * 128],
                               in_=wtok_sb[:, c * 8:(c + 1) * 8], identity=id128b_sb[:])
        t_tb = inc(ins, "sPE")
        bank[3].set_w(t_tb)
        R("wtok").add_r(t_tb)
        wait(act, [t_tb] + R("wfm").wdeps())
        pdelay(act)
        ins = act.activation(out=wfm_sb[:], in_=p3b[0:8, 0:T], func=AF.Copy)
        t_wf = inc(ins, "sA")
        R("wfm").set_w(t_wf)
        bank[3].add_r(t_wf)
        wait(pe, bank[6].wdeps() + [t_wf])
        ins = pe.matmul(out=P[6][:], lhsT=psel_sb[:], rhs=wfm_sb[:], start=True, stop=True)
        t_wbc = inc(ins, "sPE")
        bank[6].set_w(t_wbc)
        R("wfm").add_r(t_wbc)
        wait(dve, R("h").rdeps() + [t_wbc] + R("g").wdeps())
        pdelay(dve)
        ins = dve.tensor_tensor(out=g_sb[:], in0=h_sb[:], in1=P[6][:], op=ALU.mult)
        t_gg = inc(ins, "sV")
        R("g").set_w(t_gg)
        R("h").add_r(t_gg)
        bank[6].add_r(t_gg)
        return t_gg

    def fold_row(t_src, rows_t, res_name, col_off):
        wr = R("st_mean")
        wait(act, wr.wdeps() + [t_src])
        pdelay(act)
        ins = act.activation(out=wrk[0:1, 0:T], in_=P[7][0:1, 0:T], func=AF.Copy)
        t_c = inc(ins, "sA")
        wr.set_w(t_c)
        bank[7].add_r(t_c)
        wait(dve, [t_c])
        dve.tensor_tensor(out=wrk[0:1, 0:128], in0=wrk[0:1, 0:128],
                          in1=wrk[0:1, 128:256], op=ALU.add)
        ins = dve.tensor_tensor(out=wrk[0:1, 256:384], in0=wrk[0:1, 256:384],
                                in1=wrk[0:1, 384:512], op=ALU.add)
        t_mid = inc(ins, "sV")
        wait(dve, [t_mid])
        ins = dve.tensor_tensor(out=wrk[0:1, 0:128], in0=wrk[0:1, 0:128],
                                in1=wrk[0:1, 256:384], op=ALU.add)
        t_f = inc(ins, "sV")
        wr.set_w(t_f)
        rr = R(res_name)
        wait(act, [t_f] + rr.rs)
        ins = act.activation(out=rows_t[0:1, col_off:col_off + 128],
                             in_=wrk[0:1, 0:128], func=AF.Copy)
        t_o = inc(ins, "sA")
        wr.add_r(t_o)
        rr.set_w(t_o)

    def stream_block(kind, m, deps_extra=()):
        sl = m % 3
        slot = ws_slot[sl]
        if kind == "cast":
            t_l = dma(gp, "sG", ws_sb[:, sl, :, :], wbD[m], deps=slot.wdeps())
        elif kind == "scratch":
            t_l = dma(sp, "sQ", ws_sb[:, sl, :, :], wTbf[m],
                      deps=slot.wdeps() + wtb_blk[m].rdeps())
        else:
            t_l = dma(sp, "sQ", ws_sb[:, sl, :, :],
                      attnM.ap()[m * 128:(m + 1) * 128, :].rearrange(
                          "p (k f) -> p k f", k=32),
                      deps=slot.wdeps() + [t_cc])
        slot.set_w(t_l)
        return t_l

    def result_loop(kind, tile, lt, w_kind, tee, build_row, skip_mean,
                    is_last_sub, load_next, ln_next):
        t_g = R("g").w
        rbslot = lt["rb_slot"]
        sum_state = {"started": False}
        t_blk = {}
        t_blk[0] = stream_block(w_kind, 0)
        t_blk[1] = stream_block(w_kind, 1)
        for m in range(32):
            sl = m % 3
            if m + 2 < 32:
                t_blk[m + 2] = stream_block(w_kind, m + 2)
            if tee and m >= 0:
                t_t = dma(sp, "sS", wTbf[m], ws_sb[:, sl, :, :], deps=[t_blk[m]])
                ws_slot[sl].add_r(t_t)
                wtb_blk[m].set_w(t_t)
            bki = m % 2
            items = [(ws_sb[:, sl, k, :], xb_sb[:, k, :],
                      ([t_blk[m]] + xb_read_deps() if k == 0 else []))
                     for k in range(32)]
            if kind == "ffn":
                items.append((bm_sb[:, m, :], g_sb[:], [t_g, t_bm]))
            if not skip_mean:
                rsrc = wrow_sb[0:1, m * 128:(m + 1) * 128] if kind == "ffn" \
                    else acol_sb[0:1, m * 128:(m + 1) * 128]
                rres = R("wrow") if kind == "ffn" else R("acol")
                items.append((rsrc, statb[0:1, 0:T], [lt["mneg"]] + rres.rdeps()))
            bsrc = brow_sb[0:1, m * 128:(m + 1) * 128] if kind == "ffn" \
                else b2row_sb[0:1, m * 128:(m + 1) * 128]
            bres = R("brow") if kind == "ffn" else R("b2row")
            items.append((bsrc, statb[0:1, T:2 * T], [lt["sdbf"]] + bres.rdeps()))
            t_grp = mm_group(bki, P[bki][:], items)
            ws_slot[sl].add_r(t_grp)
            xb_mark_read(t_grp)
            if kind == "ffn":
                R("g").add_r(t_grp)
            if build_row:
                items2 = [(onescol_sb[:], ws_sb[:, sl, 4 * gg:4 * gg + 4, :], [])
                          for gg in range(8)]
                t_r = mm_group(7, P[7][0:1, 0:T], items2)
                ws_slot[sl].add_r(t_r)
                if kind == "ffn":
                    fold_row(t_r, wrow_sb, "wrow", m * 128)
                else:
                    fold_row(t_r, acol_sb, "acol", m * 128)
            evac(kind, m, bki, rbslot, tile, is_last_sub, load_next)
        if ln_next:
            post_ln()

    def evac(kind, m, bki, rbslot, tile, is_last_sub, load_next):
        us = u_slot[m % 4]
        bk = bank[bki]
        rbs = R("rb%d" % rbslot)
        wait(dve, bk.rdeps() + us.wdeps() + rbs.rdeps())
        pdelay(dve)
        ins = dve.tensor_tensor(out=u_sb[:, m % 4, :], in0=P[bki][:],
                                in1=rb_sb[:, rbslot, :], op=ALU.mult)
        t1 = inc(ins, "sV")
        us.set_w(t1)
        bk.add_r(t1)
        rbs.add_r(t1)
        if kind == "attn":
            us2 = u_slot[(m + 2) % 4]
            wait(act, [t1] + us2.wdeps())
            ins = act.activation(out=u_sb[:, (m + 2) % 4, :], in_=u_sb[:, m % 4, :],
                                 func=AF.Gelu)
            t2 = inc(ins, "sA")
            us2.set_w(t2)
            us.add_r(t2)
            vi, vres, vtok = (m + 2) % 4, us2, t2
        else:
            vi, vres, vtok = m % 4, us, t1
        wait(pool, [vtok] + xchunk[m].wdeps())
        ins = pool.tensor_tensor(out=x_sb[:, m, :], in0=u_sb[:, vi, :],
                                 in1=x_sb[:, m, :], op=ALU.add)
        t3 = inc(ins, "sP")
        xchunk[m].set_w(t3)
        vres.add_r(t3)
        if is_last_sub:
            t_out = dma(sp, "sS", yout[tile, m], x_sb[:, m, :], deps=[t3])
            xchunk[m].add_r(t_out)
            if load_next:
                t_ld = dma(sp, "sQ", x_sb[:, m, :], xin[tile + 1, m],
                           deps=xchunk[m].wdeps())
                xchunk[m].set_w(t_ld)

    for m in range(NC):
        xb_cast_chunk(m)
        ln_sums_chunk(m, first=(m == 0), last=(m == NC - 1))

    order = ["ffn1", "attn", "ffn2"]
    state = {"wrow": False, "acol": False, "wtbf": False, "qrows": False}

    for tile in range(ntiles):
        for si, sub in enumerate(order):
            enabled = subs_en[sub]
            rest = [s for s in order[si + 1:] if subs_en[s]]
            is_last_sub = enabled and not rest
            if not enabled:
                if sub == "ffn2" and not rest and all(not subs_en[s] for s in order):
                    pass
                continue
            kind = "ffn" if sub != "attn" else "attn"
            ln_next_kind = None
            nxt = None
            if rest:
                nxt = rest[0]
            elif tile + 1 < ntiles:
                nxt = [s for s in order if subs_en[s]][0]
            if nxt is not None:
                ln_next_kind = "ffn" if nxt != "attn" else "attn"
            need_xz = (kind == "ffn" and not state["wrow"]) or \
                      (kind == "attn" and not state["acol"])
            lt = ln_finalize(kind, t0_meanb=need_xz)
            if need_xz:
                xz_pass(lt)
            if kind == "ffn":
                small_phase(lt, skip_mean=need_xz, build_qrows=not state["qrows"])
                state["qrows"] = True
                if cfg.get("nocache_w"):
                    w_kind, tee = "cast", False
                else:
                    w_kind = "scratch" if state["wtbf"] else "cast"
                    tee = not state["wtbf"]
                build_row = not state["wrow"]
                result_loop("ffn", tile, lt, w_kind, tee, build_row, need_xz,
                            is_last_sub, is_last_sub and tile + 1 < ntiles,
                            ln_next_kind is not None)
                state["wtbf"] = True
                state["wrow"] = True
            else:
                if t_cc is None:
                    t_cc = emit_collective()
                if cfg.get("ccdelay"):
                    for _ in range(100):
                        pe.nop(cycle_cnt=20000)
                    for _ in range(100):
                        sp.nop(cycle_cnt=20000)
                build_row = not state["acol"]
                result_loop("attn", tile, lt, "attn", False, build_row, need_xz,
                            is_last_sub, is_last_sub and tile + 1 < ntiles,
                            ln_next_kind is not None)
                state["acol"] = True

    if cfg.get("dump_am"):
        amd = nc.dram_tensor("am_dump", [D, D], BF16, kind="ExternalOutput").ap()
        nc.all_engine_barrier()
        for i in range(8):
            dma(sp, "sQ", amd[i * 512:(i + 1) * 512, :], attnM.ap()[i * 512:(i + 1) * 512, :])
        wait(sp, [Tok("sQ", cnt["sQ"])])

    if cfg.get("dump"):
        dbg = nc.dram_tensor("dbg", [10, 128, 3 * T], F32, kind="ExternalOutput").ap()
        nc.all_engine_barrier()
        dma(sp, "sQ", dbg[0], stat[:, :])
        dma(gp, "sG", dbg[1][:, 0:2 * T], statb[:, :])
        dma(sp, "sQ", dbg[2][:, 0:T], rb_sb[:, 0, :])
        dma(sp, "sQ", dbg[3][:, 0:T], rb_sb[:, 1, :])
        dma(gp, "sG", dbg[4][:, 0:T], xb_sb[:, 0, :])
        dma(gp, "sG", dbg[4][:, T:2 * T], xb_sb[:, 1, :])
        dma(sp, "sQ", dbg[5][:, 0:T], x_sb[:, 0, :])
        dma(sp, "sQ", dbg[6][:, 0:T], q_sb[:, 0, :])
        dma(sp, "sQ", dbg[6][0:8, T:2 * T], sc_sb[:, :])
        dma(gp, "sG", dbg[7][:, 0:T], h_sb[:, :])
        dma(gp, "sG", dbg[7][:, T:2 * T], g_sb[:, :])
        dma(gp, "sG", dbg[8][0:8, 0:T], wfm_sb[:, :])
        dma(gp, "sG", dbg[8][:, T:T + 32], wtok_sb[:, :])
        dma(sp, "sQ", dbg[8][:, 2 * T:2 * T + 32], esc_sb[:, :])
        dma(gp, "sG", dbg[9][:, 0:T], r2s[0:128, :])
        dma(sp, "sQ", dbg[9][:, T:2 * T], u_sb[:, 0, :])

    wait(sp, [Tok("sS", cnt["sS"]), Tok("sQ", cnt["sQ"]), Tok("sG", cnt["sG"])])
    wait(gp, [Tok("sG", cnt["sG"])])
    nc.all_engine_barrier()
    es.close()
    return nc


def _layout(inputs):
    x = np.ascontiguousarray(inputs["L0_attn_x"], dtype=np.float32).reshape(16384, D)
    W = inputs["L0_ffn_W"]
    w1 = inputs["L0_attn_w1"]
    w2 = inputs["L0_attn_w2"]
    mask = inputs["L0_attn_sparse_mask"]
    Wq = inputs["L0_ffn_Wq"]
    Wk = inputs["L0_ffn_Wk"]
    A = inputs["L0_ffn_A"]
    Bm = inputs["L0_ffn_Bm"]

    def blk(MT):
        r = np.asarray(MT, dtype=np.float32).reshape(32, 128, -1, 128)
        return np.ascontiguousarray(r.transpose(2, 1, 0, 3))

    wb = blk(np.asarray(W, dtype=np.float32).T)
    w1b = blk(np.asarray(w1, dtype=np.float32))
    w2t = np.ascontiguousarray(np.asarray(w2, np.float32).T).reshape(32, 128, D)
    maskr = np.asarray(mask, np.float32).reshape(32, 128, D)
    wqb = blk(np.asarray(Wq, np.float32).T)
    wkb = blk(np.asarray(Wk, np.float32).T)
    Af = np.asarray(A, np.float32).reshape(128, D)
    ab = np.ascontiguousarray(Af.T.reshape(32, 128, 128).transpose(1, 0, 2))
    BmF = np.ascontiguousarray(np.asarray(Bm, np.float32).transpose(0, 2, 1)).reshape(128, D)
    bmb = np.ascontiguousarray(BmF.reshape(128, 32, 128))
    brow = np.asarray(inputs["L0_ffn_b"], np.float32).reshape(1, D)
    b2row = np.asarray(inputs["L0_attn_b2"], np.float32).reshape(1, D)

    in_maps = []
    for c in range(NCORE):
        xc = x[c * 2048:(c + 1) * 2048]
        xfm = np.ascontiguousarray(
            xc.reshape(NT, T, NC, 128).transpose(0, 2, 3, 1))
        in_maps.append({
            "xin": xfm, "wb": wb, "w1b": w1b,
            "w2tb": np.ascontiguousarray(w2t[:, :, c * T:(c + 1) * T]),
            "maskb": np.ascontiguousarray(maskr[:, :, c * T:(c + 1) * T]),
            "wqb": wqb, "wkb": wkb, "ab": ab, "bmb": bmb,
            "brow": brow, "b2row": b2row,
        })
    return in_maps


def kernel(**inputs):
    in_maps = _layout(inputs)
    nc = build()
    res = run_bass_kernel_spmd(nc, in_maps, core_ids=list(range(NCORE)))
    outs = []
    for c in range(NCORE):
        y = res.results[c]["yout"]
        outs.append(y.transpose(0, 3, 1, 2).reshape(2048, D))
    full = np.concatenate(outs, axis=0).reshape(2, 8192, D).astype(np.float32)
    return full


# version 02
# speedup vs baseline: 1.0182x; 1.0182x over previous
import numpy as np
import ml_dtypes
from contextlib import ExitStack

import concourse.bass as bass
import concourse.mybir as mybir
from concourse.bass_utils import run_bass_kernel_spmd

F32 = mybir.dt.float32
BF16 = mybir.dt.bfloat16
AF = mybir.ActivationFunctionType
ALU = mybir.AluOpType
AX = mybir.AxisListType
BFNP = ml_dtypes.bfloat16

NCORE = 8
D = 4096
NC = 32
T = 512
NT = 4
INV_D = 1.0 / 4096.0
EPS = 1e-5
ISQRT_DK = float(1.0 / np.sqrt(32.0))


class Tok:
    __slots__ = ("sem", "val")

    def __init__(self, sem, val):
        self.sem = sem
        self.val = val


class Res:
    __slots__ = ("w", "rs")

    def __init__(self):
        self.w = None
        self.rs = []

    def wdeps(self):
        d = list(self.rs)
        if self.w is not None:
            d.append(self.w)
        return d

    def rdeps(self):
        return [self.w] if self.w is not None else []

    def set_w(self, t):
        self.w = t
        self.rs = []

    def add_r(self, t):
        if t is not None:
            self.rs.append(t)


def _consts():
    c = {}
    c["id8"] = np.eye(8, dtype=np.float32)
    c["id128b"] = np.eye(128, dtype=BFNP)
    c["lncol"] = np.full((128, 1), INV_D, dtype=BFNP)
    c["onescol"] = np.full((128, 1), 1.0, dtype=BFNP)
    c["onesrow128"] = np.full((1, 128), 1.0, dtype=np.float32)
    c["ones8row"] = np.full((1, 8), 1.0, dtype=np.float32)
    psel = np.zeros((8, 128), dtype=BFNP)
    for e in range(8):
        psel[e, e * 16:(e + 1) * 16] = 1.0
    c["psel"] = psel
    sel = np.zeros((2, 128, 8), dtype=BFNP)
    for ch in range(2):
        for e in range(4):
            sel[ch, e * 32:(e + 1) * 32, ch * 4 + e] = 1.0
    c["sel"] = sel
    return c


def build(cfg=None):
    cfg = dict(cfg or {})
    subs_en = {"ffn1": cfg.get("ffn1", True), "attn": cfg.get("attn", True),
               "ffn2": cfg.get("ffn2", True)}
    ntiles = cfg.get("ntiles", NT)

    nc = bass.Bass("TRN2", target_bir_lowering=False, debug=False)
    es = ExitStack()

    xin = nc.dram_tensor("xin", [NT, NC, 128, T], F32, kind="ExternalInput").ap()
    wbD = nc.dram_tensor("wb", [32, 128, 32, 128], F32, kind="ExternalInput").ap()
    w1bD = nc.dram_tensor("w1b", [32, 128, 32, 128], F32, kind="ExternalInput").ap()
    w2tD = nc.dram_tensor("w2tb", [32, 128, T], F32, kind="ExternalInput").ap()
    maskD = nc.dram_tensor("maskb", [32, 128, T], F32, kind="ExternalInput").ap()
    wqD = nc.dram_tensor("wqb", [2, 128, 32, 128], F32, kind="ExternalInput").ap()
    wkD = nc.dram_tensor("wkb", [2, 128, 32, 128], F32, kind="ExternalInput").ap()
    abD = nc.dram_tensor("ab", [128, 32, 128], F32, kind="ExternalInput").ap()
    bmD = nc.dram_tensor("bmb", [128, 32, 128], F32, kind="ExternalInput").ap()
    browD = nc.dram_tensor("brow", [1, D], F32, kind="ExternalInput").ap()
    b2rowD = nc.dram_tensor("b2row", [1, D], F32, kind="ExternalInput").ap()
    yout = nc.dram_tensor("yout", [NT, NC, 128, T], F32, kind="ExternalOutput").ap()

    wTbf = nc.dram_tensor("wTbf", [32, 128, 32, 128], BF16, kind="Internal").ap()
    bar_in = nc.dram_tensor("bar_in", [1, 256], F32, kind="Internal")
    bar_out = nc.dram_tensor("bar_out", [1, 256], F32, kind="Internal")
    shard = nc.dram_tensor("attn_shard", [T, D], BF16, kind="Internal")
    attnM = nc.dram_tensor("attnM", [D, D], BF16, kind="Internal", addr_space="Shared")
    wqbf = nc.dram_tensor("wqbf", [2, 128, 32, 128], BF16, kind="Internal").ap()
    wkbf = nc.dram_tensor("wkbf", [2, 128, 32, 128], BF16, kind="Internal").ap()

    CN = _consts()
    cD = {k: nc.inline_tensor(v, name="c_" + k).ap() for k, v in CN.items()}

    S = lambda *a: es.enter_context(nc.sbuf_tensor(*a))
    x_sb = S("x", [128, NC, T], F32)
    xb_sb = S("xb", [128, NC, T], BF16)
    ws_sb = S("wstream", [128, 3, 32, 128], BF16)
    a_sb = S("a_res", [128, 32, 128], BF16)
    bm_sb = S("bm_res", [128, 32, 128], BF16)
    u_sb = S("u_t", [128, 4, T], F32)
    rb_sb = S("rstdb", [128, 2, T], F32)
    q_sb = S("q_t", [128, 2, T], F32)
    xsq_sb = S("xsq", [128, 2, T], BF16)
    prod_sb = S("prod", [128, 2, T], BF16)
    h_sb = S("h_t", [128, T], BF16)
    g_sb = S("g_t", [128, T], BF16)
    brow_sb = S("brow_sb", [128, D], BF16)
    b2row_sb = S("b2row_sb", [128, D], BF16)
    wrow_sb = S("wrow_sb", [128, D], BF16)
    acol_sb = S("acol_sb", [128, D], BF16)
    qrow_sb = S("qrow_sb", [128, 640], BF16)
    stat = S("stat", [128, 3 * T], F32)
    statb = S("statb", [128, 2 * T], BF16)
    r2s = S("r2s", [128, T], F32)
    wrk = stat
    sc_sb = S("sc", [8, T], F32)
    wfm_sb = S("wfm", [8, T], BF16)
    mx_sb = S("mx", [128, 4], F32)
    sm_sb = S("sm", [128, 8], F32)
    esc_sb = S("esc", [128, 32], F32)
    wtok_sb = S("wtok", [128, 32], BF16)
    eps_sb = S("epsc", [128, 1], F32)
    id8_sb = S("id8_sb", [8, 8], F32)
    id128b_sb = S("id128b_sb", [128, 128], BF16)
    lncol_sb = S("lncol_sb", [128, 1], BF16)
    onescol_sb = S("onescol_sb", [128, 1], BF16)
    onesrow_sb = S("onesrow_sb", [1, 128], F32)
    ones8row_sb = S("ones8row_sb", [1, 8], F32)
    psel_sb = S("psel_sb", [8, 128], BF16)
    sel_sb = S("sel_sb", [128, 2, 8], BF16)
    w2t_sb = xb_sb
    mask_sb = q_sb
    stage_sb = prod_sb

    P = [es.enter_context(nc.psum_tensor("pb%d" % i, [128, T], F32)) for i in range(8)]

    pe, act, dve, pool, sp = nc.tensor, nc.scalar, nc.vector, nc.gpsimd, nc.sync
    gp = pool

    sems, cnt, own, seen = {}, {}, {}, {}

    def mksem(name, owner=None):
        sems[name] = es.enter_context(nc.semaphore(name))
        cnt[name] = 0
        own[name] = owner

    for nm, ow in (("sQ", None), ("sS", None), ("sG", None), ("sCC", None),
                   ("sC", None), ("sGC", None),
                   ("sPE", pe), ("sA", act), ("sV", dve), ("sP", pool)):
        mksem(nm, ow)

    def inc(ins, name, n=1):
        ins.then_inc(sems[name], n)
        cnt[name] += n
        return Tok(name, cnt[name])

    def wait(eng, deps):
        best = {}
        for t in deps:
            if t is not None and t.val > best.get(t.sem, -1):
                best[t.sem] = t.val
        sn = seen.setdefault(id(eng), {})
        for s, v in best.items():
            if sn.get(s, -1) >= v:
                continue
            eng.wait_ge(sems[s], v)
            sn[s] = v

    def dma(q, qsem, out, in_, deps=()):
        wait(q, list(deps))
        return inc(q.dma_start(out=out, in_=in_), qsem, 16)

    PSUM_DELAY = int(cfg.get("delay", 0))

    def pdelay(eng):
        if PSUM_DELAY:
            eng.nop(cycle_cnt=PSUM_DELAY)

    bank = [Res() for _ in range(8)]
    ws_slot = [Res() for _ in range(3)]
    u_slot = [Res() for _ in range(4)]
    q_slot = [Res() for _ in range(2)]
    xsq_slot = [Res() for _ in range(2)]
    prod_slot = [Res() for _ in range(2)]
    mask_slot = [Res() for _ in range(2)]
    stage_slot = [Res() for _ in range(2)]
    wtb_blk = [Res() for _ in range(32)]
    xchunk = [Res() for _ in range(NC)]
    xbchunk = [Res() for _ in range(NC)]
    _misc = {}

    def R(name):
        return _misc.setdefault(name, Res())

    def xb_read_deps():
        return [xbchunk[k].w for k in range(NC)]

    def xb_mark_read(t):
        for k in range(NC):
            xbchunk[k].add_r(t)

    def mm_group(bki, out_ap, items):
        bk = bank[bki]
        ins = None
        n = len(items)
        for i, (l, r, dd) in enumerate(items):
            d = list(dd)
            if i == 0:
                d += bk.wdeps()
            wait(pe, d)
            ins = pe.matmul(out=out_ap, lhsT=l, rhs=r, start=(i == 0), stop=(i == n - 1))
        t = inc(ins, "sPE")
        bk.set_w(t)
        return t

    for dst, srcD in ((id8_sb[:], cD["id8"]), (onesrow_sb[:], cD["onesrow128"]),
                      (ones8row_sb[:], cD["ones8row"]), (psel_sb[:], cD["psel"])):
        t_constsC = dma(sp, "sC", dst, srcD)
    for dst, srcD in ((id128b_sb[:], cD["id128b"]), (lncol_sb[:], cD["lncol"]),
                      (onescol_sb[:], cD["onescol"]), (sel_sb[:, 0, :], cD["sel"][0]),
                      (sel_sb[:, 1, :], cD["sel"][1])):
        t_consts = dma(sp, "sQ", dst, srcD)
    CDEPS = [t_consts, t_constsC]
    t_brow = dma(gp, "sGC", brow_sb[0:1, :], browD)
    R("brow").set_w(t_brow)
    t_b2row = dma(gp, "sGC", b2row_sb[0:1, :], b2rowD)
    R("b2row").set_w(t_b2row)
    t_a = dma(gp, "sG", a_sb[:], abD)
    t_bm = dma(gp, "sG", bm_sb[:], bmD)
    t_wqc = dma(gp, "sG", wqbf, wqD)
    t_wkc = dma(gp, "sG", wkbf, wkD)
    t_w2t = None
    for i in range(4):
        t_w2t = dma(gp, "sG", w2t_sb[:, i * 8:(i + 1) * 8, :],
                    w2tD[i * 8:(i + 1) * 8].rearrange("k p t -> p k t"))
    for k in range(NC):
        xbchunk[k].set_w(t_w2t)
    for i in range(8):
        tk = dma(sp, "sQ", x_sb[:, i * 4:(i + 1) * 4, :],
                 xin[0, i * 4:(i + 1) * 4].rearrange("c p t -> p c t"))
        for c in range(i * 4, (i + 1) * 4):
            xchunk[c].set_w(tk)
    ins = dve.memset(eps_sb[:], EPS)
    t_eps = inc(ins, "sV")
    R("eps").set_w(t_eps)

    t_cc = None
    if subs_en["attn"]:
        for m in range(32):
            sl = m % 2
            t_w1 = dma(gp, "sG", ws_sb[:, sl, :, :], w1bD[m], deps=ws_slot[sl].wdeps())
            ws_slot[sl].set_w(t_w1)
            t_m = dma(sp, "sQ", mask_sb[:, sl, :], maskD[m], deps=q_slot[sl].wdeps())
            q_slot[sl].set_w(t_m)
            items = [(ws_sb[:, sl, k, :], w2t_sb[:, k, :],
                      ([t_w1, t_w2t] if k == 0 else [])) for k in range(32)]
            t_pe = mm_group(m % 2, P[m % 2][:], items)
            ws_slot[sl].add_r(t_pe)
            ssl = prod_slot[m % 2]
            wait(dve, [t_pe, t_m] + ssl.wdeps())
            pdelay(dve)
            ins = dve.tensor_tensor(out=stage_sb[:, m % 2, :], in0=P[m % 2][:],
                                    in1=mask_sb[:, m % 2, :], op=ALU.mult)
            t_st = inc(ins, "sV")
            ssl.set_w(t_st)
            bank[m % 2].add_r(t_st)
            q_slot[sl].add_r(t_st)
            for ml in range(4):
                t_sd = dma(sp, "sS" if False else "sQ",
                           shard.ap()[ml * 128:(ml + 1) * 128, m * 128:(m + 1) * 128],
                           stage_sb[:, m % 2, ml * 128:(ml + 1) * 128], deps=[t_st])
                ssl.add_r(t_sd)
        def emit_collective():
            wait(gp, [Tok("sQ", cnt["sQ"])])
            ins = gp.collective_compute("AllGather", ALU.bypass,
                                        replica_groups=[list(range(NCORE))],
                                        ins=[shard.ap().opt()], outs=[attnM.ap().opt()])
            inc(ins, "sCC")
            ins = gp.collective_compute("AllReduce", ALU.add,
                                        replica_groups=[list(range(NCORE))],
                                        ins=[bar_in.ap().opt()], outs=[bar_out.ap().opt()])
            return inc(ins, "sCC")
        if not cfg.get("late_cc", True):
            t_cc = emit_collective()
    if cnt["sPE"]:
        for k in range(NC):
            xbchunk[k].add_r(Tok("sPE", cnt["sPE"]))

    def xb_cast_chunk(m, deps=()):
        eng, sname = (act, "sA") if m % 2 == 0 else (pool, "sP")
        wait(eng, xchunk[m].rdeps() + xbchunk[m].wdeps() + list(deps))
        ins = eng.tensor_copy(out=xb_sb[:, m, :], in_=x_sb[:, m, :]) \
            if eng is pool else eng.activation(out=xb_sb[:, m, :], in_=x_sb[:, m, :],
                                               func=AF.Copy)
        t = inc(ins, sname)
        xbchunk[m].set_w(t)
        xchunk[m].add_r(t)
        xs = xsq_slot[m % 2]
        wait(dve, [t] + xs.wdeps())
        ins = dve.tensor_tensor(out=xsq_sb[:, m % 2, :], in0=xb_sb[:, m, :],
                                in1=xb_sb[:, m, :], op=ALU.mult)
        t2 = inc(ins, "sV")
        xs.set_w(t2)
        xbchunk[m].add_r(t2)

    def post_ln():
        for m in range(NC):
            xb_cast_chunk(m)
            ln_sums_chunk(m, first=(m == 0), last=(m == NC - 1))

    def ln_sums_chunk(m, first, last):
        d = [xbchunk[m].w, xsq_slot[m % 2].w]
        if first:
            d += bank[4].wdeps() + bank[5].wdeps() + CDEPS
        wait(pe, d)
        pe.matmul(out=P[4][0:1, :], lhsT=lncol_sb[:], rhs=xb_sb[:, m, :],
                  start=first, stop=last)
        i2 = pe.matmul(out=P[5][0:1, :], lhsT=lncol_sb[:], rhs=xsq_sb[:, m % 2, :],
                       start=first, stop=last)
        t = inc(i2, "sPE")
        xbchunk[m].add_r(t)
        xsq_slot[m % 2].add_r(t)
        if first or last:
            bank[4].set_w(t)
            bank[5].set_w(t)

    def act_row(dst_res, out_ap, in_ap, src_res, func=AF.Copy, bias=0.0, scale=1.0,
                extra=()):
        wait(act, dst_res.wdeps() + src_res.rdeps() + list(extra))
        ins = act.activation(out=out_ap, in_=in_ap, func=func, bias=bias, scale=scale)
        t = inc(ins, "sA")
        dst_res.set_w(t)
        src_res.add_r(t)
        return t

    def ln_finalize(nxt, t0_meanb=False):
        MEAN = slice(0, T)
        WORK = slice(T, 2 * T)
        RSTD = slice(2 * T, 3 * T)
        pdelay(act)
        t_mean = act_row(R("st_mean"), stat[0:1, MEAN], P[4][0:1, :], bank[4])
        t_ex2 = act_row(R("st_work"), stat[0:1, WORK], P[5][0:1, :], bank[5])
        t_mneg = act_row(R("statb_m"), statb[0:1, 0:T], stat[0:1, MEAN], R("st_mean"),
                         scale=-1.0)
        t_m2 = act_row(R("st_rstd"), stat[0:1, RSTD], stat[0:1, MEAN], R("st_mean"),
                       func=AF.Square)
        wait(dve, [t_m2, t_ex2] + R("st_work").rs + R("st_rstd").rs)
        ins = dve.tensor_tensor(out=stat[0:1, WORK], in0=stat[0:1, WORK],
                                in1=stat[0:1, RSTD], op=ALU.subtract)
        t_var = inc(ins, "sV")
        R("st_work").set_w(t_var)
        R("st_rstd").add_r(t_var)
        t_sd = act_row(R("st_work"), stat[0:1, WORK], stat[0:1, WORK], R("st_work"),
                       func=AF.Sqrt, bias=eps_sb[0:1, 0:1], extra=R("eps").rdeps())
        t_sdbf = act_row(R("statb_s"), statb[0:1, T:2 * T], stat[0:1, WORK],
                         R("st_work"))
        wait(dve, [t_sd] + R("st_rstd").wdeps())
        ins = dve.reciprocal(out=stat[0:1, RSTD], in_=stat[0:1, WORK])
        t_rstd = inc(ins, "sV")
        R("st_rstd").set_w(t_rstd)
        R("st_work").add_r(t_rstd)
        toks = {"mneg": t_mneg, "sdbf": t_sdbf}
        if nxt == "ffn":
            wait(dve, [t_rstd] + R("r2s").wdeps())
            ins = dve.tensor_tensor(out=r2s[0:1, :], in0=stat[0:1, RSTD],
                                    in1=stat[0:1, RSTD], op=ALU.mult)
            t_r2a = inc(ins, "sV")
            R("r2s").set_w(t_r2a)
            R("st_rstd").add_r(t_r2a)
            t_r2 = act_row(R("r2s"), r2s[0:1, :], r2s[0:1, :], R("r2s"),
                           scale=ISQRT_DK)
            toks["r2s"] = t_r2
            t_rh = act_row(R("st_work"), stat[0:1, WORK], stat[0:1, RSTD],
                           R("st_rstd"), scale=0.5)
            src, src_res = stat[0:1, WORK], R("st_work")
        else:
            src, src_res = stat[0:1, RSTD], R("st_rstd")
        slot = 0 if nxt == "ffn" else 1
        wait(pe, bank[7].wdeps() + src_res.rdeps() + CDEPS)
        ins = pe.matmul(out=P[7][:], lhsT=onesrow_sb[0:1, :], rhs=src,
                        start=True, stop=True)
        t_bc = inc(ins, "sPE")
        bank[7].set_w(t_bc)
        src_res.add_r(t_bc)
        rbs = R("rb%d" % slot)
        wait(act, [t_bc] + rbs.wdeps())
        pdelay(act)
        ins = act.activation(out=rb_sb[:, slot, :], in_=P[7][:], func=AF.Copy)
        t_rb = inc(ins, "sA")
        rbs.set_w(t_rb)
        bank[7].add_r(t_rb)
        toks["rb_slot"] = slot
        if t0_meanb:
            wait(pe, bank[7].wdeps() + R("st_mean").rdeps() + CDEPS)
            ins = pe.matmul(out=P[7][:], lhsT=onesrow_sb[0:1, :], rhs=stat[0:1, 0:T],
                            start=True, stop=True)
            t_bc2 = inc(ins, "sPE")
            bank[7].set_w(t_bc2)
            R("st_mean").add_r(t_bc2)
            mbslot = 1 - slot
            mb = R("rb%d" % mbslot)
            wait(act, [t_bc2] + mb.wdeps())
            pdelay(act)
            ins = act.activation(out=rb_sb[:, mbslot, :], in_=P[7][:], func=AF.Copy)
            t_mb = inc(ins, "sA")
            mb.set_w(t_mb)
            bank[7].add_r(t_mb)
            toks["mean_b"] = t_mb
            toks["mb_slot"] = mbslot
        return toks

    def xz_pass(lt):
        mbslot = lt["mb_slot"]
        for m in range(NC):
            wait(dve, xbchunk[m].wdeps() + [lt["mean_b"]])
            ins = dve.tensor_tensor(out=xb_sb[:, m, :], in0=xb_sb[:, m, :],
                                    in1=rb_sb[:, mbslot, :], op=ALU.subtract)
            t = inc(ins, "sV")
            xbchunk[m].set_w(t)
            R("rb%d" % mbslot).add_r(t)

    def small_phase(lt, skip_mean, build_qrows):
        mneg_dep = [] if skip_mean else [lt["mneg"], R("qrow").w]
        slmap = [0, 1, 2, 0]
        srcs = [(wqbf, 0, t_wqc), (wqbf, 1, t_wqc), (wkbf, 0, t_wkc), (wkbf, 1, t_wkc)]
        tw = [None] * 4
        for j in (0, 1, 2):
            srcD, mm, tc = srcs[j]
            sl = slmap[j]
            tw[j] = dma(sp, "sQ", ws_sb[:, sl, :, :], srcD[mm],
                        deps=ws_slot[sl].wdeps() + [tc])
            ws_slot[sl].set_w(tw[j])
        for j in range(4):
            if j == 3:
                srcD, mm, tc = srcs[3]
                sl = slmap[3]
                tw[3] = dma(sp, "sQ", ws_sb[:, sl, :, :], srcD[mm],
                            deps=ws_slot[sl].wdeps() + [tc])
                ws_slot[sl].set_w(tw[3])
            sl = slmap[j]
            bki = 2 + j
            items = [(ws_sb[:, sl, k, :], xb_sb[:, k, :],
                      ([tw[j]] + xb_read_deps() if k == 0 else [])) for k in range(32)]
            if not skip_mean:
                off = (0 if j < 2 else 256) + (j % 2) * 128
                items.append((qrow_sb[0:1, off:off + 128], statb[0:1, 0:T], mneg_dep))
            t_g = mm_group(bki, P[bki][:], items)
            ws_slot[sl].add_r(t_g)
            xb_mark_read(t_g)
            if build_qrows:
                items2 = [(onescol_sb[:], ws_sb[:, sl, 4 * gg:4 * gg + 4, :], [])
                          for gg in range(8)]
                t_r = mm_group(7, P[7][0:1, 0:T], items2)
                ws_slot[sl].add_r(t_r)
                off = (0 if j < 2 else 256) + (j % 2) * 128
                fold_row(t_r, qrow_sb, "qrow", off)
        items = [(a_sb[:, k, :], xb_sb[:, k, :], ([t_a] if k == 0 else []))
                 for k in range(32)]
        if not skip_mean:
            items.append((qrow_sb[0:1, 512:640], statb[0:1, 0:T], mneg_dep))
        t_h = mm_group(6, P[6][:], items)
        xb_mark_read(t_h)
        if build_qrows:
            items2 = [(onescol_sb[:], a_sb[:, 4 * gg:4 * gg + 4, :], []) for gg in range(8)]
            t_r = mm_group(7, P[7][0:1, 0:T], items2)
            fold_row(t_r, qrow_sb, "qrow", 512)
        wait(act, [t_h] + R("h").wdeps())
        pdelay(act)
        ins = act.activation(out=h_sb[:], in_=P[6][:], func=AF.Copy)
        t_hs = inc(ins, "sA")
        R("h").set_w(t_hs)
        bank[6].add_r(t_hs)
        wait(pe, bank[6].wdeps() + R("r2s").rdeps() + CDEPS)
        ins = pe.matmul(out=P[6][0:8, :], lhsT=ones8row_sb[0:1, :], rhs=r2s[0:1, :],
                        start=True, stop=True)
        t_r2b = inc(ins, "sPE")
        bank[6].set_w(t_r2b)
        R("r2s").add_r(t_r2b)
        for mm in range(2):
            qs = q_slot[mm]
            wait(act, bank[2 + mm].rdeps() + qs.wdeps())
            pdelay(act)
            ins = act.activation(out=q_sb[:, mm, :], in_=P[2 + mm][:], func=AF.Copy)
            t_q = inc(ins, "sA")
            qs.set_w(t_q)
            bank[2 + mm].add_r(t_q)
        for mm in range(2):
            ps = prod_slot[mm]
            wait(dve, q_slot[mm].rdeps() + bank[4 + mm].rdeps() + ps.wdeps())
            pdelay(dve)
            ins = dve.tensor_tensor(out=prod_sb[:, mm, :], in0=q_sb[:, mm, :],
                                    in1=P[4 + mm][:], op=ALU.mult)
            t_p = inc(ins, "sV")
            ps.set_w(t_p)
            q_slot[mm].add_r(t_p)
            bank[4 + mm].add_r(t_p)
        items = [(sel_sb[:, mm, :], prod_sb[:, mm, :], prod_slot[mm].rdeps())
                 for mm in range(2)]
        t_sc = mm_group(7, P[7][0:8, :], items)
        for mm in range(2):
            prod_slot[mm].add_r(t_sc)
        wait(act, [t_sc] + R("scs").wdeps())
        pdelay(act)
        ins = act.activation(out=sc_sb[:], in_=P[7][0:8, :], func=AF.Copy)
        t_s1 = inc(ins, "sA")
        R("scs").set_w(t_s1)
        bank[7].add_r(t_s1)
        wait(dve, [t_s1, t_r2b])
        pdelay(dve)
        ins = dve.tensor_tensor(out=sc_sb[:], in0=sc_sb[:], in1=P[6][0:8, :],
                                op=ALU.mult)
        t_s2 = inc(ins, "sV")
        R("scs").set_w(t_s2)
        bank[6].add_r(t_s2)
        wait(pe, bank[2].wdeps() + [t_s2] + CDEPS)
        for c in range(4):
            ins = pe.transpose(out=P[2][:, c * 8:(c + 1) * 8],
                               in_=sc_sb[0:8, c * 128:(c + 1) * 128], identity=id8_sb[:])
        t_tr = inc(ins, "sPE")
        bank[2].set_w(t_tr)
        R("scs").add_r(t_tr)
        wait(dve, [t_tr] + R("mx").wdeps())
        pdelay(dve)
        for c in range(4):
            ins = dve.tensor_reduce(out=mx_sb[:, c:c + 1], in_=P[2][:, c * 8:(c + 1) * 8],
                                    axis=AX.X, op=ALU.max, negate=True)
        t_mx = inc(ins, "sV")
        R("mx").set_w(t_mx)
        bank[2].add_r(t_mx)
        wait(act, [t_mx] + R("esc").wdeps())
        pdelay(act)
        for c in range(4):
            ins = act.activation(out=esc_sb[:, c * 8:(c + 1) * 8],
                                 in_=P[2][:, c * 8:(c + 1) * 8], func=AF.Exp,
                                 bias=mx_sb[:, c:c + 1], scale=1.0)
        t_e = inc(ins, "sA")
        R("esc").set_w(t_e)
        R("mx").add_r(t_e)
        bank[2].add_r(t_e)
        wait(dve, [t_e] + R("sm").wdeps())
        for c in range(4):
            ins = dve.tensor_reduce(out=sm_sb[:, c:c + 1], in_=esc_sb[:, c * 8:(c + 1) * 8],
                                    axis=AX.X, op=ALU.add)
        t_sums = inc(ins, "sV")
        wait(dve, [t_sums])
        ins = dve.reciprocal(out=sm_sb[:, 4:8], in_=sm_sb[:, 0:4])
        t_rec = inc(ins, "sV")
        wait(dve, [t_rec])
        for c in range(4):
            ins = dve.tensor_scalar(out=wtok_sb[:, c * 8:(c + 1) * 8],
                                    in0=esc_sb[:, c * 8:(c + 1) * 8],
                                    scalar1=sm_sb[:, 4 + c:5 + c], scalar2=None,
                                    op0=ALU.mult)
        t_w = inc(ins, "sV")
        R("sm").set_w(t_w)
        R("esc").add_r(t_w)
        R("wtok").set_w(t_w)
        p3b = P[3][:].bitcast(BF16)
        wait(pe, bank[3].wdeps() + [t_w])
        for c in range(4):
            ins = pe.transpose(out=p3b[0:8, c * 128:(c + 1) * 128],
                               in_=wtok_sb[:, c * 8:(c + 1) * 8], identity=id128b_sb[:])
        t_tb = inc(ins, "sPE")
        bank[3].set_w(t_tb)
        R("wtok").add_r(t_tb)
        wait(act, [t_tb] + R("wfm").wdeps())
        pdelay(act)
        ins = act.activation(out=wfm_sb[:], in_=p3b[0:8, 0:T], func=AF.Copy)
        t_wf = inc(ins, "sA")
        R("wfm").set_w(t_wf)
        bank[3].add_r(t_wf)
        wait(pe, bank[6].wdeps() + [t_wf] + CDEPS)
        ins = pe.matmul(out=P[6][:], lhsT=psel_sb[:], rhs=wfm_sb[:], start=True, stop=True)
        t_wbc = inc(ins, "sPE")
        bank[6].set_w(t_wbc)
        R("wfm").add_r(t_wbc)
        wait(dve, R("h").rdeps() + [t_wbc] + R("g").wdeps())
        pdelay(dve)
        ins = dve.tensor_tensor(out=g_sb[:], in0=h_sb[:], in1=P[6][:], op=ALU.mult)
        t_gg = inc(ins, "sV")
        R("g").set_w(t_gg)
        R("h").add_r(t_gg)
        bank[6].add_r(t_gg)
        return t_gg

    def fold_row(t_src, rows_t, res_name, col_off):
        wr = R("st_mean")
        wait(act, wr.wdeps() + [t_src])
        pdelay(act)
        ins = act.activation(out=wrk[0:1, 0:T], in_=P[7][0:1, 0:T], func=AF.Copy)
        t_c = inc(ins, "sA")
        wr.set_w(t_c)
        bank[7].add_r(t_c)
        wait(dve, [t_c])
        dve.tensor_tensor(out=wrk[0:1, 0:128], in0=wrk[0:1, 0:128],
                          in1=wrk[0:1, 128:256], op=ALU.add)
        ins = dve.tensor_tensor(out=wrk[0:1, 256:384], in0=wrk[0:1, 256:384],
                                in1=wrk[0:1, 384:512], op=ALU.add)
        t_mid = inc(ins, "sV")
        wait(dve, [t_mid])
        ins = dve.tensor_tensor(out=wrk[0:1, 0:128], in0=wrk[0:1, 0:128],
                                in1=wrk[0:1, 256:384], op=ALU.add)
        t_f = inc(ins, "sV")
        wr.set_w(t_f)
        rr = R(res_name)
        wait(act, [t_f] + rr.rs)
        ins = act.activation(out=rows_t[0:1, col_off:col_off + 128],
                             in_=wrk[0:1, 0:128], func=AF.Copy)
        t_o = inc(ins, "sA")
        wr.add_r(t_o)
        rr.set_w(t_o)

    def stream_block(kind, m, deps_extra=()):
        sl = m % 3
        slot = ws_slot[sl]
        if kind == "cast":
            t_l = dma(gp, "sG", ws_sb[:, sl, :, :], wbD[m], deps=slot.wdeps())
        elif kind == "scratch":
            t_l = dma(sp, "sQ", ws_sb[:, sl, :, :], wTbf[m],
                      deps=slot.wdeps() + wtb_blk[m].rdeps())
        else:
            t_l = dma(sp, "sQ", ws_sb[:, sl, :, :],
                      attnM.ap()[m * 128:(m + 1) * 128, :].rearrange(
                          "p (k f) -> p k f", k=32),
                      deps=slot.wdeps() + [t_cc])
        slot.set_w(t_l)
        return t_l

    def result_loop(kind, tile, lt, w_kind, tee, build_row, skip_mean,
                    is_last_sub, load_next, ln_next):
        t_g = R("g").w
        rbslot = lt["rb_slot"]
        sum_state = {"started": False}
        t_blk = {}
        t_blk[0] = stream_block(w_kind, 0)
        t_blk[1] = stream_block(w_kind, 1)
        for m in range(32):
            sl = m % 3
            if m + 2 < 32:
                t_blk[m + 2] = stream_block(w_kind, m + 2)
            if tee and m >= 0:
                t_t = dma(sp, "sS", wTbf[m], ws_sb[:, sl, :, :], deps=[t_blk[m]])
                ws_slot[sl].add_r(t_t)
                wtb_blk[m].set_w(t_t)
            bki = m % 2
            items = [(ws_sb[:, sl, k, :], xb_sb[:, k, :],
                      ([t_blk[m]] + xb_read_deps() if k == 0 else []))
                     for k in range(32)]
            if kind == "ffn":
                items.append((bm_sb[:, m, :], g_sb[:], [t_g, t_bm]))
            if not skip_mean:
                rsrc = wrow_sb[0:1, m * 128:(m + 1) * 128] if kind == "ffn" \
                    else acol_sb[0:1, m * 128:(m + 1) * 128]
                rres = R("wrow") if kind == "ffn" else R("acol")
                items.append((rsrc, statb[0:1, 0:T], [lt["mneg"]] + rres.rdeps()))
            bsrc = brow_sb[0:1, m * 128:(m + 1) * 128] if kind == "ffn" \
                else b2row_sb[0:1, m * 128:(m + 1) * 128]
            bres = R("brow") if kind == "ffn" else R("b2row")
            items.append((bsrc, statb[0:1, T:2 * T], [lt["sdbf"]] + bres.rdeps()))
            t_grp = mm_group(bki, P[bki][:], items)
            ws_slot[sl].add_r(t_grp)
            xb_mark_read(t_grp)
            if kind == "ffn":
                R("g").add_r(t_grp)
            if build_row:
                items2 = [(onescol_sb[:], ws_sb[:, sl, 4 * gg:4 * gg + 4, :], [])
                          for gg in range(8)]
                t_r = mm_group(7, P[7][0:1, 0:T], items2)
                ws_slot[sl].add_r(t_r)
                if kind == "ffn":
                    fold_row(t_r, wrow_sb, "wrow", m * 128)
                else:
                    fold_row(t_r, acol_sb, "acol", m * 128)
            evac(kind, m, bki, rbslot, tile, is_last_sub, load_next)
        if ln_next:
            post_ln()

    def evac(kind, m, bki, rbslot, tile, is_last_sub, load_next):
        us = u_slot[m % 4]
        bk = bank[bki]
        rbs = R("rb%d" % rbslot)
        wait(dve, bk.rdeps() + us.wdeps() + rbs.rdeps())
        pdelay(dve)
        ins = dve.tensor_tensor(out=u_sb[:, m % 4, :], in0=P[bki][:],
                                in1=rb_sb[:, rbslot, :], op=ALU.mult)
        t1 = inc(ins, "sV")
        us.set_w(t1)
        bk.add_r(t1)
        rbs.add_r(t1)
        if kind == "attn":
            us2 = u_slot[(m + 2) % 4]
            wait(act, [t1] + us2.wdeps())
            ins = act.activation(out=u_sb[:, (m + 2) % 4, :], in_=u_sb[:, m % 4, :],
                                 func=AF.Gelu)
            t2 = inc(ins, "sA")
            us2.set_w(t2)
            us.add_r(t2)
            vi, vres, vtok = (m + 2) % 4, us2, t2
        else:
            vi, vres, vtok = m % 4, us, t1
        wait(pool, [vtok] + xchunk[m].wdeps())
        ins = pool.tensor_tensor(out=x_sb[:, m, :], in0=u_sb[:, vi, :],
                                 in1=x_sb[:, m, :], op=ALU.add)
        t3 = inc(ins, "sP")
        xchunk[m].set_w(t3)
        vres.add_r(t3)
        if is_last_sub:
            t_out = dma(sp, "sS", yout[tile, m], x_sb[:, m, :], deps=[t3])
            xchunk[m].add_r(t_out)
            if load_next:
                t_ld = dma(sp, "sQ", x_sb[:, m, :], xin[tile + 1, m],
                           deps=xchunk[m].wdeps())
                xchunk[m].set_w(t_ld)

    for m in range(NC):
        xb_cast_chunk(m)
        ln_sums_chunk(m, first=(m == 0), last=(m == NC - 1))

    order = ["ffn1", "attn", "ffn2"]
    state = {"wrow": False, "acol": False, "wtbf": False, "qrows": False}

    for tile in range(ntiles):
        for si, sub in enumerate(order):
            enabled = subs_en[sub]
            rest = [s for s in order[si + 1:] if subs_en[s]]
            is_last_sub = enabled and not rest
            if not enabled:
                if sub == "ffn2" and not rest and all(not subs_en[s] for s in order):
                    pass
                continue
            kind = "ffn" if sub != "attn" else "attn"
            ln_next_kind = None
            nxt = None
            if rest:
                nxt = rest[0]
            elif tile + 1 < ntiles:
                nxt = [s for s in order if subs_en[s]][0]
            if nxt is not None:
                ln_next_kind = "ffn" if nxt != "attn" else "attn"
            need_xz = (kind == "ffn" and not state["wrow"]) or \
                      (kind == "attn" and not state["acol"])
            lt = ln_finalize(kind, t0_meanb=need_xz)
            if need_xz:
                xz_pass(lt)
            if kind == "ffn":
                small_phase(lt, skip_mean=need_xz, build_qrows=not state["qrows"])
                state["qrows"] = True
                if cfg.get("nocache_w"):
                    w_kind, tee = "cast", False
                else:
                    w_kind = "scratch" if state["wtbf"] else "cast"
                    tee = not state["wtbf"]
                build_row = not state["wrow"]
                result_loop("ffn", tile, lt, w_kind, tee, build_row, need_xz,
                            is_last_sub, is_last_sub and tile + 1 < ntiles,
                            ln_next_kind is not None)
                state["wtbf"] = True
                state["wrow"] = True
            else:
                if t_cc is None:
                    t_cc = emit_collective()
                if cfg.get("ccdelay"):
                    for _ in range(100):
                        pe.nop(cycle_cnt=20000)
                    for _ in range(100):
                        sp.nop(cycle_cnt=20000)
                build_row = not state["acol"]
                result_loop("attn", tile, lt, "attn", False, build_row, need_xz,
                            is_last_sub, is_last_sub and tile + 1 < ntiles,
                            ln_next_kind is not None)
                state["acol"] = True

    if cfg.get("dump_am"):
        amd = nc.dram_tensor("am_dump", [D, D], BF16, kind="ExternalOutput").ap()
        nc.all_engine_barrier()
        for i in range(8):
            dma(sp, "sQ", amd[i * 512:(i + 1) * 512, :], attnM.ap()[i * 512:(i + 1) * 512, :])
        wait(sp, [Tok("sQ", cnt["sQ"])])

    if cfg.get("dump"):
        dbg = nc.dram_tensor("dbg", [10, 128, 3 * T], F32, kind="ExternalOutput").ap()
        nc.all_engine_barrier()
        dma(sp, "sQ", dbg[0], stat[:, :])
        dma(gp, "sG", dbg[1][:, 0:2 * T], statb[:, :])
        dma(sp, "sQ", dbg[2][:, 0:T], rb_sb[:, 0, :])
        dma(sp, "sQ", dbg[3][:, 0:T], rb_sb[:, 1, :])
        dma(gp, "sG", dbg[4][:, 0:T], xb_sb[:, 0, :])
        dma(gp, "sG", dbg[4][:, T:2 * T], xb_sb[:, 1, :])
        dma(sp, "sQ", dbg[5][:, 0:T], x_sb[:, 0, :])
        dma(sp, "sQ", dbg[6][:, 0:T], q_sb[:, 0, :])
        dma(sp, "sQ", dbg[6][0:8, T:2 * T], sc_sb[:, :])
        dma(gp, "sG", dbg[7][:, 0:T], h_sb[:, :])
        dma(gp, "sG", dbg[7][:, T:2 * T], g_sb[:, :])
        dma(gp, "sG", dbg[8][0:8, 0:T], wfm_sb[:, :])
        dma(gp, "sG", dbg[8][:, T:T + 32], wtok_sb[:, :])
        dma(sp, "sQ", dbg[8][:, 2 * T:2 * T + 32], esc_sb[:, :])
        dma(gp, "sG", dbg[9][:, 0:T], r2s[0:128, :])
        dma(sp, "sQ", dbg[9][:, T:2 * T], u_sb[:, 0, :])

    wait(sp, [Tok("sS", cnt["sS"]), Tok("sQ", cnt["sQ"]), Tok("sG", cnt["sG"])])
    wait(gp, [Tok("sG", cnt["sG"])])
    nc.all_engine_barrier()
    es.close()
    return nc


def _layout(inputs):
    x = np.ascontiguousarray(inputs["L0_attn_x"], dtype=np.float32).reshape(16384, D)
    W = inputs["L0_ffn_W"]
    w1 = inputs["L0_attn_w1"]
    w2 = inputs["L0_attn_w2"]
    mask = inputs["L0_attn_sparse_mask"]
    Wq = inputs["L0_ffn_Wq"]
    Wk = inputs["L0_ffn_Wk"]
    A = inputs["L0_ffn_A"]
    Bm = inputs["L0_ffn_Bm"]

    def blk(MT):
        r = np.asarray(MT, dtype=np.float32).reshape(32, 128, -1, 128)
        return np.ascontiguousarray(r.transpose(2, 1, 0, 3))

    wb = blk(np.asarray(W, dtype=np.float32).T)
    w1b = blk(np.asarray(w1, dtype=np.float32))
    w2t = np.ascontiguousarray(np.asarray(w2, np.float32).T).reshape(32, 128, D)
    maskr = np.asarray(mask, np.float32).reshape(32, 128, D)
    wqb = blk(np.asarray(Wq, np.float32).T)
    wkb = blk(np.asarray(Wk, np.float32).T)
    Af = np.asarray(A, np.float32).reshape(128, D)
    ab = np.ascontiguousarray(Af.T.reshape(32, 128, 128).transpose(1, 0, 2))
    BmF = np.ascontiguousarray(np.asarray(Bm, np.float32).transpose(0, 2, 1)).reshape(128, D)
    bmb = np.ascontiguousarray(BmF.reshape(128, 32, 128))
    brow = np.asarray(inputs["L0_ffn_b"], np.float32).reshape(1, D)
    b2row = np.asarray(inputs["L0_attn_b2"], np.float32).reshape(1, D)

    in_maps = []
    for c in range(NCORE):
        xc = x[c * 2048:(c + 1) * 2048]
        xfm = np.ascontiguousarray(
            xc.reshape(NT, T, NC, 128).transpose(0, 2, 3, 1))
        in_maps.append({
            "xin": xfm, "wb": wb, "w1b": w1b,
            "w2tb": np.ascontiguousarray(w2t[:, :, c * T:(c + 1) * T]),
            "maskb": np.ascontiguousarray(maskr[:, :, c * T:(c + 1) * T]),
            "wqb": wqb, "wkb": wkb, "ab": ab, "bmb": bmb,
            "brow": brow, "b2row": b2row,
        })
    return in_maps


def kernel(**inputs):
    in_maps = _layout(inputs)
    nc = build()
    res = run_bass_kernel_spmd(nc, in_maps, core_ids=list(range(NCORE)))
    outs = []
    for c in range(NCORE):
        y = res.results[c]["yout"]
        outs.append(y.transpose(0, 3, 1, 2).reshape(2048, D))
    full = np.concatenate(outs, axis=0).reshape(2, 8192, D).astype(np.float32)
    return full


# version 04
# speedup vs baseline: 1.2129x; 1.1912x over previous
import numpy as np
import ml_dtypes
from contextlib import ExitStack

import concourse.bass as bass
import concourse.mybir as mybir
from concourse.bass_utils import run_bass_kernel_spmd

F32 = mybir.dt.float32
BF16 = mybir.dt.bfloat16
AF = mybir.ActivationFunctionType
ALU = mybir.AluOpType
AX = mybir.AxisListType
BFNP = ml_dtypes.bfloat16

NCORE = 8
D = 4096
NC = 32
T = 512
NT = 4
INV_D = 1.0 / 4096.0
EPS = 1e-5
ISQRT_DK = float(1.0 / np.sqrt(32.0))


class Tok:
    __slots__ = ("sem", "val")

    def __init__(self, sem, val):
        self.sem = sem
        self.val = val


class Res:
    __slots__ = ("w", "rs")

    def __init__(self):
        self.w = None
        self.rs = []

    def wdeps(self):
        d = list(self.rs)
        if self.w is not None:
            d.append(self.w)
        return d

    def rdeps(self):
        return [self.w] if self.w is not None else []

    def set_w(self, t):
        self.w = t
        self.rs = []

    def add_r(self, t):
        if t is not None:
            self.rs.append(t)


def _consts():
    c = {}
    c["id8"] = np.eye(8, dtype=np.float32)
    c["id128b"] = np.eye(128, dtype=BFNP)
    c["lncol"] = np.full((128, 1), INV_D, dtype=BFNP)
    c["onescol"] = np.full((128, 1), 1.0, dtype=BFNP)
    c["onesrow128"] = np.full((1, 128), 1.0, dtype=np.float32)
    c["ones8row"] = np.full((1, 8), 1.0, dtype=np.float32)
    psel = np.zeros((8, 128), dtype=BFNP)
    for e in range(8):
        psel[e, e * 16:(e + 1) * 16] = 1.0
    c["psel"] = psel
    sel = np.zeros((2, 128, 8), dtype=BFNP)
    for ch in range(2):
        for e in range(4):
            sel[ch, e * 32:(e + 1) * 32, ch * 4 + e] = 1.0
    c["sel"] = sel
    return c


def build(cfg=None):
    cfg = dict(cfg or {})
    subs_en = {"ffn1": cfg.get("ffn1", True), "attn": cfg.get("attn", True),
               "ffn2": cfg.get("ffn2", True)}
    ntiles = cfg.get("ntiles", NT)

    nc = bass.Bass("TRN2", target_bir_lowering=False, debug=False)
    es = ExitStack()

    xin = nc.dram_tensor("xin", [NT, NC, 128, T], F32, kind="ExternalInput").ap()
    wbD = nc.dram_tensor("wb", [32, 128, 32, 128], F32, kind="ExternalInput").ap()
    NW1 = 8 if cfg.get("banded", True) else 32
    w1bD = nc.dram_tensor("w1b", [NW1, 128, 32, 128], F32, kind="ExternalInput").ap()
    w2tD = nc.dram_tensor("w2tb", [32, 128, T], F32, kind="ExternalInput").ap()
    maskD = nc.dram_tensor("maskb", [NW1, 128, T], F32, kind="ExternalInput").ap()
    wqD = nc.dram_tensor("wqb", [2, 128, 32, 128], F32, kind="ExternalInput").ap()
    wkD = nc.dram_tensor("wkb", [2, 128, 32, 128], F32, kind="ExternalInput").ap()
    abD = nc.dram_tensor("ab", [128, 32, 128], F32, kind="ExternalInput").ap()
    bmD = nc.dram_tensor("bmb", [128, 32, 128], F32, kind="ExternalInput").ap()
    browD = nc.dram_tensor("brow", [1, D], F32, kind="ExternalInput").ap()
    b2rowD = nc.dram_tensor("b2row", [1, D], F32, kind="ExternalInput").ap()
    yout = nc.dram_tensor("yout", [NT, NC, 128, T], F32, kind="ExternalOutput").ap()

    wTbf = nc.dram_tensor("wTbf", [32, 128, 32, 128], BF16, kind="Internal").ap()
    bar_in = nc.dram_tensor("bar_in", [1, 256], F32, kind="Internal")
    bar_out = nc.dram_tensor("bar_out", [1, 256], F32, kind="Internal")
    BANDED = bool(cfg.get("banded", True))
    KW = 8
    kwin_lo = [min(max(4 * r - 2, 0), 32 - KW) for r in range(NCORE)]
    SHW = KW * 128 if BANDED else D
    shard = nc.dram_tensor("attn_shard", [T, SHW], BF16, kind="Internal")
    attnM = nc.dram_tensor("attnM", [D, SHW], BF16, kind="Internal", addr_space="Shared")
    wqbf = nc.dram_tensor("wqbf", [2, 128, 32, 128], BF16, kind="Internal").ap()
    wkbf = nc.dram_tensor("wkbf", [2, 128, 32, 128], BF16, kind="Internal").ap()

    CN = _consts()
    cD = {k: nc.inline_tensor(v, name="c_" + k).ap() for k, v in CN.items()}

    S = lambda *a: es.enter_context(nc.sbuf_tensor(*a))
    x_sb = S("x", [128, NC, T], F32)
    xb_sb = S("xb", [128, NC, T], BF16)
    ws_sb = S("wstream", [128, 3, 32, 128], BF16)
    a_sb = S("a_res", [128, 32, 128], BF16)
    bm_sb = S("bm_res", [128, 32, 128], BF16)
    u_sb = S("u_t", [128, 4, T], F32)
    rb_sb = S("rstdb", [128, 2, T], F32)
    q_sb = S("q_t", [128, 2, T], F32)
    xsq_sb = S("xsq", [128, 2, T], BF16)
    prod_sb = S("prod", [128, 2, T], BF16)
    h_sb = S("h_t", [128, T], BF16)
    g_sb = S("g_t", [128, T], BF16)
    brow_sb = S("brow_sb", [128, D], BF16)
    b2row_sb = S("b2row_sb", [128, D], BF16)
    wrow_sb = S("wrow_sb", [128, D], BF16)
    acol_sb = S("acol_sb", [128, D], BF16)
    qrow_sb = S("qrow_sb", [128, 640], BF16)
    stat = S("stat", [128, 3 * T], F32)
    statb = S("statb", [128, 2 * T], BF16)
    r2s = S("r2s", [128, T], F32)
    wrk = stat
    sc_sb = S("sc", [8, T], F32)
    wfm_sb = S("wfm", [8, T], BF16)
    mx_sb = S("mx", [128, 4], F32)
    sm_sb = S("sm", [128, 8], F32)
    esc_sb = S("esc", [128, 32], F32)
    wtok_sb = S("wtok", [128, 32], BF16)
    eps_sb = S("epsc", [128, 1], F32)
    id8_sb = S("id8_sb", [8, 8], F32)
    id128b_sb = S("id128b_sb", [128, 128], BF16)
    lncol_sb = S("lncol_sb", [128, 1], BF16)
    onescol_sb = S("onescol_sb", [128, 1], BF16)
    onesrow_sb = S("onesrow_sb", [1, 128], F32)
    ones8row_sb = S("ones8row_sb", [1, 8], F32)
    psel_sb = S("psel_sb", [8, 128], BF16)
    sel_sb = S("sel_sb", [128, 2, 8], BF16)
    w2t_sb = xb_sb
    mask_sb = q_sb
    stage_sb = prod_sb

    P = [es.enter_context(nc.psum_tensor("pb%d" % i, [128, T], F32)) for i in range(8)]

    pe, act, dve, pool, sp = nc.tensor, nc.scalar, nc.vector, nc.gpsimd, nc.sync
    gp = pool

    sems, cnt, own, seen = {}, {}, {}, {}

    def mksem(name, owner=None):
        sems[name] = es.enter_context(nc.semaphore(name))
        cnt[name] = 0
        own[name] = owner

    for nm, ow in (("sQ", None), ("sS", None), ("sG", None), ("sCC", None),
                   ("sC", None), ("sGC", None),
                   ("sPE", pe), ("sA", act), ("sV", dve), ("sP", pool)):
        mksem(nm, ow)

    def inc(ins, name, n=1):
        ins.then_inc(sems[name], n)
        cnt[name] += n
        return Tok(name, cnt[name])

    def wait(eng, deps):
        best = {}
        for t in deps:
            if t is not None and t.val > best.get(t.sem, -1):
                best[t.sem] = t.val
        sn = seen.setdefault(id(eng), {})
        for s, v in best.items():
            if sn.get(s, -1) >= v:
                continue
            eng.wait_ge(sems[s], v)
            sn[s] = v

    def dma(q, qsem, out, in_, deps=()):
        wait(q, list(deps))
        return inc(q.dma_start(out=out, in_=in_), qsem, 16)

    PSUM_DELAY = int(cfg.get("delay", 0))

    def pdelay(eng):
        if PSUM_DELAY:
            eng.nop(cycle_cnt=PSUM_DELAY)

    bank = [Res() for _ in range(8)]
    ws_slot = [Res() for _ in range(3)]
    u_slot = [Res() for _ in range(4)]
    q_slot = [Res() for _ in range(2)]
    xsq_slot = [Res() for _ in range(2)]
    prod_slot = [Res() for _ in range(2)]
    mask_slot = [Res() for _ in range(2)]
    stage_slot = [Res() for _ in range(2)]
    wtb_blk = [Res() for _ in range(32)]
    xchunk = [Res() for _ in range(NC)]
    xbchunk = [Res() for _ in range(NC)]
    _misc = {}

    def R(name):
        return _misc.setdefault(name, Res())

    def xb_read_deps():
        return [xbchunk[k].w for k in range(NC)]

    def xb_mark_read(t):
        for k in range(NC):
            xbchunk[k].add_r(t)

    def mm_group(bki, out_ap, items):
        bk = bank[bki]
        ins = None
        n = len(items)
        for i, (l, r, dd) in enumerate(items):
            d = list(dd)
            if i == 0:
                d += bk.wdeps()
            wait(pe, d)
            ins = pe.matmul(out=out_ap, lhsT=l, rhs=r, start=(i == 0), stop=(i == n - 1))
        t = inc(ins, "sPE")
        bk.set_w(t)
        return t

    for dst, srcD in ((id8_sb[:], cD["id8"]), (onesrow_sb[:], cD["onesrow128"]),
                      (ones8row_sb[:], cD["ones8row"]), (psel_sb[:], cD["psel"])):
        t_constsC = dma(sp, "sC", dst, srcD)
    for dst, srcD in ((id128b_sb[:], cD["id128b"]), (lncol_sb[:], cD["lncol"]),
                      (onescol_sb[:], cD["onescol"]), (sel_sb[:, 0, :], cD["sel"][0]),
                      (sel_sb[:, 1, :], cD["sel"][1])):
        t_consts = dma(sp, "sQ", dst, srcD)
    CDEPS = [t_consts, t_constsC]
    t_brow = dma(gp, "sGC", brow_sb[0:1, :], browD)
    R("brow").set_w(t_brow)
    t_b2row = dma(gp, "sGC", b2row_sb[0:1, :], b2rowD)
    R("b2row").set_w(t_b2row)
    t_a = dma(gp, "sG", a_sb[:], abD)
    t_bm = dma(gp, "sG", bm_sb[:], bmD)
    t_wqc = dma(gp, "sG", wqbf, wqD)
    t_wkc = dma(gp, "sG", wkbf, wkD)
    t_w2t = None
    for i in range(4):
        t_w2t = dma(gp, "sG", w2t_sb[:, i * 8:(i + 1) * 8, :],
                    w2tD[i * 8:(i + 1) * 8].rearrange("k p t -> p k t"))
    for k in range(NC):
        xbchunk[k].set_w(t_w2t)
    for i in range(8):
        tk = dma(sp, "sQ", x_sb[:, i * 4:(i + 1) * 4, :],
                 xin[0, i * 4:(i + 1) * 4].rearrange("c p t -> p c t"))
        for c in range(i * 4, (i + 1) * 4):
            xchunk[c].set_w(tk)
    ins = dve.memset(eps_sb[:], EPS)
    t_eps = inc(ins, "sV")
    R("eps").set_w(t_eps)

    t_cc = None
    core_rank = None
    if subs_en["attn"]:
        pre_ms = range(32)
        if BANDED:
            pre_ms = range(KW)
        for j_ms, m in enumerate(pre_ms):
            sl = m % 2
            t_w1 = dma(gp, "sG", ws_sb[:, sl, :, :], w1bD[j_ms if BANDED else m],
                       deps=ws_slot[sl].wdeps())
            ws_slot[sl].set_w(t_w1)
            t_m = dma(sp, "sQ", mask_sb[:, sl, :], maskD[j_ms if BANDED else m],
                      deps=q_slot[sl].wdeps())
            q_slot[sl].set_w(t_m)
            items = [(ws_sb[:, sl, k, :], w2t_sb[:, k, :],
                      ([t_w1, t_w2t] if k == 0 else [])) for k in range(32)]
            t_pe = mm_group(m % 2, P[m % 2][:], items)
            ws_slot[sl].add_r(t_pe)
            ssl = prod_slot[m % 2]
            wait(dve, [t_pe, t_m] + ssl.wdeps())
            pdelay(dve)
            ins = dve.tensor_tensor(out=stage_sb[:, m % 2, :], in0=P[m % 2][:],
                                    in1=mask_sb[:, m % 2, :], op=ALU.mult)
            t_st = inc(ins, "sV")
            ssl.set_w(t_st)
            bank[m % 2].add_r(t_st)
            q_slot[sl].add_r(t_st)
            colbase = (j_ms if BANDED else m) * 128
            for ml in range(4):
                t_sd = dma(sp, "sS" if False else "sQ",
                           shard.ap()[ml * 128:(ml + 1) * 128, colbase:colbase + 128],
                           stage_sb[:, m % 2, ml * 128:(ml + 1) * 128], deps=[t_st])
                ssl.add_r(t_sd)
        def emit_collective():
            wait(gp, [Tok("sQ", cnt["sQ"])])
            ins = gp.collective_compute("AllGather", ALU.bypass,
                                        replica_groups=[list(range(NCORE))],
                                        ins=[shard.ap().opt()], outs=[attnM.ap().opt()])
            inc(ins, "sCC")
            ins = gp.collective_compute("AllReduce", ALU.add,
                                        replica_groups=[list(range(NCORE))],
                                        ins=[bar_in.ap().opt()], outs=[bar_out.ap().opt()])
            t_b = inc(ins, "sCC")
            wait(sp, [t_b])
            ins = None
            for _ in range(12):
                ins = sp.nop(cycle_cnt=20000)
            return inc(ins, "sC")
        if not cfg.get("late_cc", True):
            t_cc = emit_collective()
    if cnt["sPE"]:
        for k in range(NC):
            xbchunk[k].add_r(Tok("sPE", cnt["sPE"]))

    def xb_cast_chunk(m, deps=()):
        eng, sname = (act, "sA") if m % 2 == 0 else (pool, "sP")
        wait(eng, xchunk[m].rdeps() + xbchunk[m].wdeps() + list(deps))
        ins = eng.tensor_copy(out=xb_sb[:, m, :], in_=x_sb[:, m, :]) \
            if eng is pool else eng.activation(out=xb_sb[:, m, :], in_=x_sb[:, m, :],
                                               func=AF.Copy)
        t = inc(ins, sname)
        xbchunk[m].set_w(t)
        xchunk[m].add_r(t)
        xs = xsq_slot[m % 2]
        wait(dve, [t] + xs.wdeps())
        ins = dve.tensor_tensor(out=xsq_sb[:, m % 2, :], in0=xb_sb[:, m, :],
                                in1=xb_sb[:, m, :], op=ALU.mult)
        t2 = inc(ins, "sV")
        xs.set_w(t2)
        xbchunk[m].add_r(t2)

    def post_ln():
        for m in range(NC):
            xb_cast_chunk(m)
            ln_sums_chunk(m, first=(m == 0), last=(m == NC - 1))

    def ln_sums_chunk(m, first, last):
        d = [xbchunk[m].w, xsq_slot[m % 2].w]
        if first:
            d += bank[4].wdeps() + bank[5].wdeps() + CDEPS
        wait(pe, d)
        pe.matmul(out=P[4][0:1, :], lhsT=lncol_sb[:], rhs=xb_sb[:, m, :],
                  start=first, stop=last)
        i2 = pe.matmul(out=P[5][0:1, :], lhsT=lncol_sb[:], rhs=xsq_sb[:, m % 2, :],
                       start=first, stop=last)
        t = inc(i2, "sPE")
        xbchunk[m].add_r(t)
        xsq_slot[m % 2].add_r(t)
        if first or last:
            bank[4].set_w(t)
            bank[5].set_w(t)

    def act_row(dst_res, out_ap, in_ap, src_res, func=AF.Copy, bias=0.0, scale=1.0,
                extra=()):
        wait(act, dst_res.wdeps() + src_res.rdeps() + list(extra))
        ins = act.activation(out=out_ap, in_=in_ap, func=func, bias=bias, scale=scale)
        t = inc(ins, "sA")
        dst_res.set_w(t)
        src_res.add_r(t)
        return t

    def ln_finalize(nxt, t0_meanb=False):
        MEAN = slice(0, T)
        WORK = slice(T, 2 * T)
        RSTD = slice(2 * T, 3 * T)
        pdelay(act)
        t_mean = act_row(R("st_mean"), stat[0:1, MEAN], P[4][0:1, :], bank[4])
        t_ex2 = act_row(R("st_work"), stat[0:1, WORK], P[5][0:1, :], bank[5])
        t_mneg = act_row(R("statb_m"), statb[0:1, 0:T], stat[0:1, MEAN], R("st_mean"),
                         scale=-1.0)
        t_m2 = act_row(R("st_rstd"), stat[0:1, RSTD], stat[0:1, MEAN], R("st_mean"),
                       func=AF.Square)
        wait(dve, [t_m2, t_ex2] + R("st_work").rs + R("st_rstd").rs)
        ins = dve.tensor_tensor(out=stat[0:1, WORK], in0=stat[0:1, WORK],
                                in1=stat[0:1, RSTD], op=ALU.subtract)
        t_var = inc(ins, "sV")
        R("st_work").set_w(t_var)
        R("st_rstd").add_r(t_var)
        t_sd = act_row(R("st_work"), stat[0:1, WORK], stat[0:1, WORK], R("st_work"),
                       func=AF.Sqrt, bias=eps_sb[0:1, 0:1], extra=R("eps").rdeps())
        t_sdbf = act_row(R("statb_s"), statb[0:1, T:2 * T], stat[0:1, WORK],
                         R("st_work"))
        wait(dve, [t_sd] + R("st_rstd").wdeps())
        ins = dve.reciprocal(out=stat[0:1, RSTD], in_=stat[0:1, WORK])
        t_rstd = inc(ins, "sV")
        R("st_rstd").set_w(t_rstd)
        R("st_work").add_r(t_rstd)
        toks = {"mneg": t_mneg, "sdbf": t_sdbf}
        if nxt == "ffn":
            wait(dve, [t_rstd] + R("r2s").wdeps())
            ins = dve.tensor_tensor(out=r2s[0:1, :], in0=stat[0:1, RSTD],
                                    in1=stat[0:1, RSTD], op=ALU.mult)
            t_r2a = inc(ins, "sV")
            R("r2s").set_w(t_r2a)
            R("st_rstd").add_r(t_r2a)
            t_r2 = act_row(R("r2s"), r2s[0:1, :], r2s[0:1, :], R("r2s"),
                           scale=ISQRT_DK)
            toks["r2s"] = t_r2
            t_rh = act_row(R("st_work"), stat[0:1, WORK], stat[0:1, RSTD],
                           R("st_rstd"), scale=0.5)
            src, src_res = stat[0:1, WORK], R("st_work")
        else:
            src, src_res = stat[0:1, RSTD], R("st_rstd")
        slot = 0 if nxt == "ffn" else 1
        wait(pe, bank[7].wdeps() + src_res.rdeps() + CDEPS)
        ins = pe.matmul(out=P[7][:], lhsT=onesrow_sb[0:1, :], rhs=src,
                        start=True, stop=True)
        t_bc = inc(ins, "sPE")
        bank[7].set_w(t_bc)
        src_res.add_r(t_bc)
        rbs = R("rb%d" % slot)
        wait(act, [t_bc] + rbs.wdeps())
        pdelay(act)
        ins = act.activation(out=rb_sb[:, slot, :], in_=P[7][:], func=AF.Copy)
        t_rb = inc(ins, "sA")
        rbs.set_w(t_rb)
        bank[7].add_r(t_rb)
        toks["rb_slot"] = slot
        if t0_meanb:
            wait(pe, bank[7].wdeps() + R("st_mean").rdeps() + CDEPS)
            ins = pe.matmul(out=P[7][:], lhsT=onesrow_sb[0:1, :], rhs=stat[0:1, 0:T],
                            start=True, stop=True)
            t_bc2 = inc(ins, "sPE")
            bank[7].set_w(t_bc2)
            R("st_mean").add_r(t_bc2)
            mbslot = 1 - slot
            mb = R("rb%d" % mbslot)
            wait(act, [t_bc2] + mb.wdeps())
            pdelay(act)
            ins = act.activation(out=rb_sb[:, mbslot, :], in_=P[7][:], func=AF.Copy)
            t_mb = inc(ins, "sA")
            mb.set_w(t_mb)
            bank[7].add_r(t_mb)
            toks["mean_b"] = t_mb
            toks["mb_slot"] = mbslot
        return toks

    def xz_pass(lt):
        mbslot = lt["mb_slot"]
        for m in range(NC):
            wait(dve, xbchunk[m].wdeps() + [lt["mean_b"]])
            ins = dve.tensor_tensor(out=xb_sb[:, m, :], in0=xb_sb[:, m, :],
                                    in1=rb_sb[:, mbslot, :], op=ALU.subtract)
            t = inc(ins, "sV")
            xbchunk[m].set_w(t)
            R("rb%d" % mbslot).add_r(t)

    def small_phase(lt, skip_mean, build_qrows):
        mneg_dep = [] if skip_mean else [lt["mneg"], R("qrow").w]
        slmap = [0, 1, 2, 0]
        srcs = [(wqbf, 0, t_wqc), (wqbf, 1, t_wqc), (wkbf, 0, t_wkc), (wkbf, 1, t_wkc)]
        tw = [None] * 4
        for j in (0, 1, 2):
            srcD, mm, tc = srcs[j]
            sl = slmap[j]
            tw[j] = dma(sp, "sQ", ws_sb[:, sl, :, :], srcD[mm],
                        deps=ws_slot[sl].wdeps() + [tc])
            ws_slot[sl].set_w(tw[j])
        for j in range(4):
            if j == 3:
                srcD, mm, tc = srcs[3]
                sl = slmap[3]
                tw[3] = dma(sp, "sQ", ws_sb[:, sl, :, :], srcD[mm],
                            deps=ws_slot[sl].wdeps() + [tc])
                ws_slot[sl].set_w(tw[3])
            sl = slmap[j]
            bki = 2 + j
            items = [(ws_sb[:, sl, k, :], xb_sb[:, k, :],
                      ([tw[j]] + xb_read_deps() if k == 0 else [])) for k in range(32)]
            if not skip_mean:
                off = (0 if j < 2 else 256) + (j % 2) * 128
                items.append((qrow_sb[0:1, off:off + 128], statb[0:1, 0:T], mneg_dep))
            t_g = mm_group(bki, P[bki][:], items)
            ws_slot[sl].add_r(t_g)
            xb_mark_read(t_g)
            if build_qrows:
                items2 = [(onescol_sb[:], ws_sb[:, sl, 4 * gg:4 * gg + 4, :], [])
                          for gg in range(8)]
                t_r = mm_group(7, P[7][0:1, 0:T], items2)
                ws_slot[sl].add_r(t_r)
                off = (0 if j < 2 else 256) + (j % 2) * 128
                fold_row(t_r, qrow_sb, "qrow", off)
        items = [(a_sb[:, k, :], xb_sb[:, k, :], ([t_a] if k == 0 else []))
                 for k in range(32)]
        if not skip_mean:
            items.append((qrow_sb[0:1, 512:640], statb[0:1, 0:T], mneg_dep))
        t_h = mm_group(6, P[6][:], items)
        xb_mark_read(t_h)
        if build_qrows:
            items2 = [(onescol_sb[:], a_sb[:, 4 * gg:4 * gg + 4, :], []) for gg in range(8)]
            t_r = mm_group(7, P[7][0:1, 0:T], items2)
            fold_row(t_r, qrow_sb, "qrow", 512)
        wait(act, [t_h] + R("h").wdeps())
        pdelay(act)
        ins = act.activation(out=h_sb[:], in_=P[6][:], func=AF.Copy)
        t_hs = inc(ins, "sA")
        R("h").set_w(t_hs)
        bank[6].add_r(t_hs)
        wait(pe, bank[6].wdeps() + R("r2s").rdeps() + CDEPS)
        ins = pe.matmul(out=P[6][0:8, :], lhsT=ones8row_sb[0:1, :], rhs=r2s[0:1, :],
                        start=True, stop=True)
        t_r2b = inc(ins, "sPE")
        bank[6].set_w(t_r2b)
        R("r2s").add_r(t_r2b)
        for mm in range(2):
            qs = q_slot[mm]
            wait(act, bank[2 + mm].rdeps() + qs.wdeps())
            pdelay(act)
            ins = act.activation(out=q_sb[:, mm, :], in_=P[2 + mm][:], func=AF.Copy)
            t_q = inc(ins, "sA")
            qs.set_w(t_q)
            bank[2 + mm].add_r(t_q)
        for mm in range(2):
            ps = prod_slot[mm]
            wait(dve, q_slot[mm].rdeps() + bank[4 + mm].rdeps() + ps.wdeps())
            pdelay(dve)
            ins = dve.tensor_tensor(out=prod_sb[:, mm, :], in0=q_sb[:, mm, :],
                                    in1=P[4 + mm][:], op=ALU.mult)
            t_p = inc(ins, "sV")
            ps.set_w(t_p)
            q_slot[mm].add_r(t_p)
            bank[4 + mm].add_r(t_p)
        items = [(sel_sb[:, mm, :], prod_sb[:, mm, :], prod_slot[mm].rdeps())
                 for mm in range(2)]
        t_sc = mm_group(7, P[7][0:8, :], items)
        for mm in range(2):
            prod_slot[mm].add_r(t_sc)
        wait(act, [t_sc] + R("scs").wdeps())
        pdelay(act)
        ins = act.activation(out=sc_sb[:], in_=P[7][0:8, :], func=AF.Copy)
        t_s1 = inc(ins, "sA")
        R("scs").set_w(t_s1)
        bank[7].add_r(t_s1)
        wait(dve, [t_s1, t_r2b])
        pdelay(dve)
        ins = dve.tensor_tensor(out=sc_sb[:], in0=sc_sb[:], in1=P[6][0:8, :],
                                op=ALU.mult)
        t_s2 = inc(ins, "sV")
        R("scs").set_w(t_s2)
        bank[6].add_r(t_s2)
        wait(pe, bank[2].wdeps() + [t_s2] + CDEPS)
        for c in range(4):
            ins = pe.transpose(out=P[2][:, c * 8:(c + 1) * 8],
                               in_=sc_sb[0:8, c * 128:(c + 1) * 128], identity=id8_sb[:])
        t_tr = inc(ins, "sPE")
        bank[2].set_w(t_tr)
        R("scs").add_r(t_tr)
        wait(dve, [t_tr] + R("mx").wdeps())
        pdelay(dve)
        for c in range(4):
            ins = dve.tensor_reduce(out=mx_sb[:, c:c + 1], in_=P[2][:, c * 8:(c + 1) * 8],
                                    axis=AX.X, op=ALU.max, negate=True)
        t_mx = inc(ins, "sV")
        R("mx").set_w(t_mx)
        bank[2].add_r(t_mx)
        wait(act, [t_mx] + R("esc").wdeps())
        pdelay(act)
        for c in range(4):
            ins = act.activation(out=esc_sb[:, c * 8:(c + 1) * 8],
                                 in_=P[2][:, c * 8:(c + 1) * 8], func=AF.Exp,
                                 bias=mx_sb[:, c:c + 1], scale=1.0)
        t_e = inc(ins, "sA")
        R("esc").set_w(t_e)
        R("mx").add_r(t_e)
        bank[2].add_r(t_e)
        wait(dve, [t_e] + R("sm").wdeps())
        for c in range(4):
            ins = dve.tensor_reduce(out=sm_sb[:, c:c + 1], in_=esc_sb[:, c * 8:(c + 1) * 8],
                                    axis=AX.X, op=ALU.add)
        t_sums = inc(ins, "sV")
        wait(dve, [t_sums])
        ins = dve.reciprocal(out=sm_sb[:, 4:8], in_=sm_sb[:, 0:4])
        t_rec = inc(ins, "sV")
        wait(dve, [t_rec])
        for c in range(4):
            ins = dve.tensor_scalar(out=wtok_sb[:, c * 8:(c + 1) * 8],
                                    in0=esc_sb[:, c * 8:(c + 1) * 8],
                                    scalar1=sm_sb[:, 4 + c:5 + c], scalar2=None,
                                    op0=ALU.mult)
        t_w = inc(ins, "sV")
        R("sm").set_w(t_w)
        R("esc").add_r(t_w)
        R("wtok").set_w(t_w)
        p3b = P[3][:].bitcast(BF16)
        wait(pe, bank[3].wdeps() + [t_w])
        for c in range(4):
            ins = pe.transpose(out=p3b[0:8, c * 128:(c + 1) * 128],
                               in_=wtok_sb[:, c * 8:(c + 1) * 8], identity=id128b_sb[:])
        t_tb = inc(ins, "sPE")
        bank[3].set_w(t_tb)
        R("wtok").add_r(t_tb)
        wait(act, [t_tb] + R("wfm").wdeps())
        pdelay(act)
        ins = act.activation(out=wfm_sb[:], in_=p3b[0:8, 0:T], func=AF.Copy)
        t_wf = inc(ins, "sA")
        R("wfm").set_w(t_wf)
        bank[3].add_r(t_wf)
        wait(pe, bank[6].wdeps() + [t_wf] + CDEPS)
        ins = pe.matmul(out=P[6][:], lhsT=psel_sb[:], rhs=wfm_sb[:], start=True, stop=True)
        t_wbc = inc(ins, "sPE")
        bank[6].set_w(t_wbc)
        R("wfm").add_r(t_wbc)
        wait(dve, R("h").rdeps() + [t_wbc] + R("g").wdeps())
        pdelay(dve)
        ins = dve.tensor_tensor(out=g_sb[:], in0=h_sb[:], in1=P[6][:], op=ALU.mult)
        t_gg = inc(ins, "sV")
        R("g").set_w(t_gg)
        R("h").add_r(t_gg)
        bank[6].add_r(t_gg)
        return t_gg

    def fold_row(t_src, rows_t, res_name, col_off):
        wr = R("st_mean")
        wait(act, wr.wdeps() + [t_src])
        pdelay(act)
        ins = act.activation(out=wrk[0:1, 0:T], in_=P[7][0:1, 0:T], func=AF.Copy)
        t_c = inc(ins, "sA")
        wr.set_w(t_c)
        bank[7].add_r(t_c)
        wait(dve, [t_c])
        dve.tensor_tensor(out=wrk[0:1, 0:128], in0=wrk[0:1, 0:128],
                          in1=wrk[0:1, 128:256], op=ALU.add)
        ins = dve.tensor_tensor(out=wrk[0:1, 256:384], in0=wrk[0:1, 256:384],
                                in1=wrk[0:1, 384:512], op=ALU.add)
        t_mid = inc(ins, "sV")
        wait(dve, [t_mid])
        ins = dve.tensor_tensor(out=wrk[0:1, 0:128], in0=wrk[0:1, 0:128],
                                in1=wrk[0:1, 256:384], op=ALU.add)
        t_f = inc(ins, "sV")
        wr.set_w(t_f)
        rr = R(res_name)
        wait(act, [t_f] + rr.rs)
        ins = act.activation(out=rows_t[0:1, col_off:col_off + 128],
                             in_=wrk[0:1, 0:128], func=AF.Copy)
        t_o = inc(ins, "sA")
        wr.add_r(t_o)
        rr.set_w(t_o)

    def stream_block(kind, m, deps_extra=()):
        sl = m % 3
        slot = ws_slot[sl]
        if kind == "cast":
            t_l = dma(gp, "sG", ws_sb[:, sl, :, :], wbD[m], deps=slot.wdeps())
        elif kind == "scratch":
            t_l = dma(sp, "sQ", ws_sb[:, sl, :, :], wTbf[m],
                      deps=slot.wdeps() + wtb_blk[m].rdeps())
        else:
            if BANDED:
                lo = kwin_lo[m // 4]
                b0 = max(m - 2, 0)
                b1 = min(m + 2, 31)
                nb = b1 - b0 + 1
                t_l = dma(sp, "sQ", ws_sb[:, sl, 0:nb, :],
                          attnM.ap()[m * 128:(m + 1) * 128,
                                     (b0 - lo) * 128:(b1 + 1 - lo) * 128].rearrange(
                              "p (k f) -> p k f", k=nb),
                          deps=slot.wdeps() + [t_cc])
            else:
                t_l = dma(sp, "sQ", ws_sb[:, sl, :, :],
                          attnM.ap()[m * 128:(m + 1) * 128, :].rearrange(
                              "p (k f) -> p k f", k=32),
                          deps=slot.wdeps() + [t_cc])
        slot.set_w(t_l)
        return t_l

    def result_loop(kind, tile, lt, w_kind, tee, build_row, skip_mean,
                    is_last_sub, load_next, ln_next):
        t_g = R("g").w
        rbslot = lt["rb_slot"]
        sum_state = {"started": False}
        t_blk = {}
        t_blk[0] = stream_block(w_kind, 0)
        t_blk[1] = stream_block(w_kind, 1)
        for m in range(32):
            sl = m % 3
            if m + 2 < 32:
                t_blk[m + 2] = stream_block(w_kind, m + 2)
            if tee and m >= 0:
                t_t = dma(sp, "sS", wTbf[m], ws_sb[:, sl, :, :], deps=[t_blk[m]])
                ws_slot[sl].add_r(t_t)
                wtb_blk[m].set_w(t_t)
            bki = m % 2
            if kind == "attn" and BANDED:
                b0 = max(m - 2, 0)
                b1 = min(m + 2, 31)
                items = [(ws_sb[:, sl, j, :], xb_sb[:, b0 + j, :],
                          ([t_blk[m]] + xb_read_deps() if j == 0 else []))
                         for j in range(b1 - b0 + 1)]
            else:
                items = [(ws_sb[:, sl, k, :], xb_sb[:, k, :],
                          ([t_blk[m]] + xb_read_deps() if k == 0 else []))
                         for k in range(32)]
            if kind == "ffn":
                items.append((bm_sb[:, m, :], g_sb[:], [t_g, t_bm]))
            if not skip_mean:
                rsrc = wrow_sb[0:1, m * 128:(m + 1) * 128] if kind == "ffn" \
                    else acol_sb[0:1, m * 128:(m + 1) * 128]
                rres = R("wrow") if kind == "ffn" else R("acol")
                items.append((rsrc, statb[0:1, 0:T], [lt["mneg"]] + rres.rdeps()))
            bsrc = brow_sb[0:1, m * 128:(m + 1) * 128] if kind == "ffn" \
                else b2row_sb[0:1, m * 128:(m + 1) * 128]
            bres = R("brow") if kind == "ffn" else R("b2row")
            items.append((bsrc, statb[0:1, T:2 * T], [lt["sdbf"]] + bres.rdeps()))
            t_grp = mm_group(bki, P[bki][:], items)
            ws_slot[sl].add_r(t_grp)
            xb_mark_read(t_grp)
            if kind == "ffn":
                R("g").add_r(t_grp)
            if build_row:
                if kind == "attn" and BANDED:
                    b0 = max(m - 2, 0)
                    b1 = min(m + 2, 31)
                    items2 = [(onescol_sb[:], ws_sb[:, sl, j, :], CDEPS)
                              for j in range(b1 - b0 + 1)]
                    t_r = mm_group(7, P[7][0:1, 0:128], items2)
                    ws_slot[sl].add_r(t_r)
                    rr = R("acol")
                    wait(act, [t_r] + rr.rs)
                    ins = act.activation(out=acol_sb[0:1, m * 128:(m + 1) * 128],
                                         in_=P[7][0:1, 0:128], func=AF.Copy)
                    t_o = inc(ins, "sA")
                    bank[7].add_r(t_o)
                    rr.set_w(t_o)
                else:
                    items2 = [(onescol_sb[:], ws_sb[:, sl, 4 * gg:4 * gg + 4, :], [])
                              for gg in range(8)]
                    t_r = mm_group(7, P[7][0:1, 0:T], items2)
                    ws_slot[sl].add_r(t_r)
                    if kind == "ffn":
                        fold_row(t_r, wrow_sb, "wrow", m * 128)
                    else:
                        fold_row(t_r, acol_sb, "acol", m * 128)
            evac(kind, m, bki, rbslot, tile, is_last_sub, load_next)
        if ln_next:
            post_ln()

    def evac(kind, m, bki, rbslot, tile, is_last_sub, load_next):
        us = u_slot[m % 4]
        bk = bank[bki]
        rbs = R("rb%d" % rbslot)
        wait(dve, bk.rdeps() + us.wdeps() + rbs.rdeps())
        pdelay(dve)
        ins = dve.tensor_tensor(out=u_sb[:, m % 4, :], in0=P[bki][:],
                                in1=rb_sb[:, rbslot, :], op=ALU.mult)
        t1 = inc(ins, "sV")
        us.set_w(t1)
        bk.add_r(t1)
        rbs.add_r(t1)
        if kind == "attn":
            us2 = u_slot[(m + 2) % 4]
            wait(act, [t1] + us2.wdeps())
            ins = act.activation(out=u_sb[:, (m + 2) % 4, :], in_=u_sb[:, m % 4, :],
                                 func=AF.Gelu)
            t2 = inc(ins, "sA")
            us2.set_w(t2)
            us.add_r(t2)
            vi, vres, vtok = (m + 2) % 4, us2, t2
        else:
            vi, vres, vtok = m % 4, us, t1
        wait(pool, [vtok] + xchunk[m].wdeps())
        ins = pool.tensor_tensor(out=x_sb[:, m, :], in0=u_sb[:, vi, :],
                                 in1=x_sb[:, m, :], op=ALU.add)
        t3 = inc(ins, "sP")
        xchunk[m].set_w(t3)
        vres.add_r(t3)
        if is_last_sub:
            t_out = dma(sp, "sS", yout[tile, m], x_sb[:, m, :], deps=[t3])
            xchunk[m].add_r(t_out)
            if load_next:
                t_ld = dma(sp, "sQ", x_sb[:, m, :], xin[tile + 1, m],
                           deps=xchunk[m].wdeps())
                xchunk[m].set_w(t_ld)

    for m in range(NC):
        xb_cast_chunk(m)
        ln_sums_chunk(m, first=(m == 0), last=(m == NC - 1))

    order = ["ffn1", "attn", "ffn2"]
    state = {"wrow": False, "acol": False, "wtbf": False, "qrows": False}

    for tile in range(ntiles):
        for si, sub in enumerate(order):
            enabled = subs_en[sub]
            rest = [s for s in order[si + 1:] if subs_en[s]]
            is_last_sub = enabled and not rest
            if not enabled:
                if sub == "ffn2" and not rest and all(not subs_en[s] for s in order):
                    pass
                continue
            kind = "ffn" if sub != "attn" else "attn"
            ln_next_kind = None
            nxt = None
            if rest:
                nxt = rest[0]
            elif tile + 1 < ntiles:
                nxt = [s for s in order if subs_en[s]][0]
            if nxt is not None:
                ln_next_kind = "ffn" if nxt != "attn" else "attn"
            need_xz = (kind == "ffn" and not state["wrow"]) or \
                      (kind == "attn" and not state["acol"])
            lt = ln_finalize(kind, t0_meanb=need_xz)
            if need_xz:
                xz_pass(lt)
            if kind == "ffn":
                small_phase(lt, skip_mean=need_xz, build_qrows=not state["qrows"])
                state["qrows"] = True
                if cfg.get("nocache_w"):
                    w_kind, tee = "cast", False
                else:
                    w_kind = "scratch" if state["wtbf"] else "cast"
                    tee = not state["wtbf"]
                build_row = not state["wrow"]
                result_loop("ffn", tile, lt, w_kind, tee, build_row, need_xz,
                            is_last_sub, is_last_sub and tile + 1 < ntiles,
                            ln_next_kind is not None)
                state["wtbf"] = True
                state["wrow"] = True
            else:
                if t_cc is None:
                    t_cc = emit_collective()
                if cfg.get("ccdelay"):
                    for _ in range(100):
                        pe.nop(cycle_cnt=20000)
                    for _ in range(100):
                        sp.nop(cycle_cnt=20000)
                build_row = not state["acol"]
                result_loop("attn", tile, lt, "attn", False, build_row, need_xz,
                            is_last_sub, is_last_sub and tile + 1 < ntiles,
                            ln_next_kind is not None)
                state["acol"] = True

    if cfg.get("dump_am"):
        amd = nc.dram_tensor("am_dump", [D, SHW], BF16, kind="ExternalOutput").ap()
        nc.all_engine_barrier()
        for i in range(8):
            dma(sp, "sQ", amd[i * 512:(i + 1) * 512, :], attnM.ap()[i * 512:(i + 1) * 512, :])
        wait(sp, [Tok("sQ", cnt["sQ"])])

    if cfg.get("dump"):
        dbg = nc.dram_tensor("dbg", [10, 128, 3 * T], F32, kind="ExternalOutput").ap()
        nc.all_engine_barrier()
        dma(sp, "sQ", dbg[0], stat[:, :])
        dma(gp, "sG", dbg[1][:, 0:2 * T], statb[:, :])
        dma(sp, "sQ", dbg[2][:, 0:T], rb_sb[:, 0, :])
        dma(sp, "sQ", dbg[3][:, 0:T], rb_sb[:, 1, :])
        dma(gp, "sG", dbg[4][:, 0:T], xb_sb[:, 0, :])
        dma(gp, "sG", dbg[4][:, T:2 * T], xb_sb[:, 1, :])
        dma(sp, "sQ", dbg[5][:, 0:T], x_sb[:, 0, :])
        dma(sp, "sQ", dbg[6][:, 0:T], q_sb[:, 0, :])
        dma(sp, "sQ", dbg[6][0:8, T:2 * T], sc_sb[:, :])
        dma(gp, "sG", dbg[7][:, 0:T], h_sb[:, :])
        dma(gp, "sG", dbg[7][:, T:2 * T], g_sb[:, :])
        dma(gp, "sG", dbg[8][0:8, 0:T], wfm_sb[:, :])
        dma(gp, "sG", dbg[8][:, T:T + 32], wtok_sb[:, :])
        dma(sp, "sQ", dbg[8][:, 2 * T:2 * T + 32], esc_sb[:, :])
        dma(gp, "sG", dbg[9][:, 0:T], r2s[0:128, :])
        dma(sp, "sQ", dbg[9][:, T:2 * T], u_sb[:, 0, :])

    wait(sp, [Tok("sS", cnt["sS"]), Tok("sQ", cnt["sQ"]), Tok("sG", cnt["sG"])])
    wait(gp, [Tok("sG", cnt["sG"])])
    nc.all_engine_barrier()
    es.close()
    return nc


def _layout(inputs, banded=True):
    x = np.ascontiguousarray(inputs["L0_attn_x"], dtype=np.float32).reshape(16384, D)
    W = inputs["L0_ffn_W"]
    w1 = inputs["L0_attn_w1"]
    w2 = inputs["L0_attn_w2"]
    mask = inputs["L0_attn_sparse_mask"]
    Wq = inputs["L0_ffn_Wq"]
    Wk = inputs["L0_ffn_Wk"]
    A = inputs["L0_ffn_A"]
    Bm = inputs["L0_ffn_Bm"]

    def blk(MT):
        r = np.asarray(MT, dtype=np.float32).reshape(32, 128, -1, 128)
        return np.ascontiguousarray(r.transpose(2, 1, 0, 3))

    wb = blk(np.asarray(W, dtype=np.float32).T)
    w1b = blk(np.asarray(w1, dtype=np.float32))
    w2t = np.ascontiguousarray(np.asarray(w2, np.float32).T).reshape(32, 128, D)
    maskr = np.asarray(mask, np.float32).reshape(32, 128, D)
    wqb = blk(np.asarray(Wq, np.float32).T)
    wkb = blk(np.asarray(Wk, np.float32).T)
    Af = np.asarray(A, np.float32).reshape(128, D)
    ab = np.ascontiguousarray(Af.T.reshape(32, 128, 128).transpose(1, 0, 2))
    BmF = np.ascontiguousarray(np.asarray(Bm, np.float32).transpose(0, 2, 1)).reshape(128, D)
    bmb = np.ascontiguousarray(BmF.reshape(128, 32, 128))
    brow = np.asarray(inputs["L0_ffn_b"], np.float32).reshape(1, D)
    b2row = np.asarray(inputs["L0_attn_b2"], np.float32).reshape(1, D)

    KW = 8
    kwin_lo = [min(max(4 * r - 2, 0), 32 - KW) for r in range(NCORE)]
    in_maps = []
    for c in range(NCORE):
        xc = x[c * 2048:(c + 1) * 2048]
        xfm = np.ascontiguousarray(
            xc.reshape(NT, T, NC, 128).transpose(0, 2, 3, 1))
        if banded:
            lo = kwin_lo[c]
            w1bc = np.ascontiguousarray(w1b[lo:lo + KW])
            maskc = np.ascontiguousarray(maskr[lo:lo + KW, :, c * T:(c + 1) * T])
        else:
            w1bc = w1b
            maskc = np.ascontiguousarray(maskr[:, :, c * T:(c + 1) * T])
        in_maps.append({
            "xin": xfm, "wb": wb, "w1b": w1bc,
            "w2tb": np.ascontiguousarray(w2t[:, :, c * T:(c + 1) * T]),
            "maskb": maskc,
            "wqb": wqb, "wkb": wkb, "ab": ab, "bmb": bmb,
            "brow": brow, "b2row": b2row,
        })
    return in_maps


def kernel(**inputs):
    in_maps = _layout(inputs)
    nc = build()
    res = run_bass_kernel_spmd(nc, in_maps, core_ids=list(range(NCORE)))
    outs = []
    for c in range(NCORE):
        y = res.results[c]["yout"]
        outs.append(y.transpose(0, 3, 1, 2).reshape(2048, D))
    full = np.concatenate(outs, axis=0).reshape(2, 8192, D).astype(np.float32)
    return full
